# version 15
# baseline (speedup 1.0000x reference)
import numpy as np
import ml_dtypes
from contextlib import ExitStack

import concourse.bass as bass
import concourse.mybir as mybir
import concourse.bass_utils as _bass_utils
from concourse.bass_utils import run_bass_kernel_spmd

if not getattr(_bass_utils, "_ldw_opt_patched", False):
    _orig_run_command = _bass_utils.run_command

    def _run_command_ldw(argv, **kwargs):
        argv = ["--enable-ldw-opt=true" if a == "--enable-ldw-opt=false" else a
                for a in argv]
        return _orig_run_command(argv, **kwargs)

    _bass_utils.run_command = _run_command_ldw
    _bass_utils._ldw_opt_patched = True

F32 = mybir.dt.float32
BF16 = mybir.dt.bfloat16
I32 = mybir.dt.int32
AF = mybir.ActivationFunctionType
ALU = mybir.AluOpType
AX = mybir.AxisListType

NCORES = 8
QF = 512
EPS = 1e-5

FULL_CFG = dict(S=8192, HEADS=32, I=8192)


class Sem:
    def __init__(self, ctx, nc, name):
        self.h = ctx.enter_context(nc.semaphore(name))
        self.n = 0

    def inc(self, inst, k=1):
        inst.then_inc(self.h, k)
        self.n += k
        return self.n

    def dma(self, inst):
        return self.inc(inst, 16)


def build(cfg):
    S = cfg["S"]; HEADS = cfg["HEADS"]; I = cfg["I"]
    H = 64 * HEADS
    T = S // NCORES
    FB = H // 128
    HC = H // 128
    KC = S // 128
    KCQ = KC // 4
    SQ = S // 4
    QH = T // QF
    NG = HEADS // 4
    TT = T // 128
    IB = I // 128
    ELEN = 2 * S - 1
    WLEN = S + T - 1
    ROWP = WLEN + 129
    GD = 4
    NW = (WLEN + 511) // 512
    FQN = H // 512
    NB4 = FB // 4
    NTH2 = T // 512
    assert T % QF == 0 and KC % 4 == 0 and H % 512 == 0 and FB % 4 == 0

    nc = bass.Bass("TRN2", target_bir_lowering=False)

    x_in = nc.dram_tensor("x", [T, H], F32, kind="ExternalInput")
    wq_t = nc.dram_tensor("wq_t", [FB, 128, H], F32, kind="ExternalInput")
    wk_t = nc.dram_tensor("wk_t", [FB, 128, H], F32, kind="ExternalInput")
    wv_t = nc.dram_tensor("wv_t", [FB, 128, H], F32, kind="ExternalInput")
    wo_t2 = nc.dram_tensor("wo_t2", [H, H], F32, kind="ExternalInput")
    rel = nc.dram_tensor("rel_pos", [1023, 64], F32, kind="ExternalInput")
    gate_t = nc.dram_tensor("gate_t", [IB, 128, H], I32, kind="ExternalInput")
    up_t = nc.dram_tensor("up_t", [IB, 128, H], I32, kind="ExternalInput")
    down_t = nc.dram_tensor("down_t", [IB, 128, H], I32, kind="ExternalInput")
    gate_s = nc.dram_tensor("gate_s", [128, IB], F32, kind="ExternalInput")
    up_s = nc.dram_tensor("up_s", [128, IB], F32, kind="ExternalInput")
    down_s = nc.dram_tensor("down_s", [1, H], F32, kind="ExternalInput")
    ident_in = nc.dram_tensor("ident", [128, 128], BF16, kind="ExternalInput")
    anti_in = nc.dram_tensor("anti", [128, 128], BF16, kind="ExternalInput")
    sel_in = nc.dram_tensor("sel", [8, 128], BF16, kind="ExternalInput")

    out_d = nc.dram_tensor("out", [T, H], F32, kind="ExternalOutput")

    kc_d = nc.dram_tensor("kc_d", [H, T], BF16)
    vc_d = nc.dram_tensor("vc_d", [T, 65 * HEADS], BF16)
    kg_d = nc.dram_tensor("kg_d", [8 * H, T], BF16, addr_space="Shared")
    vg_d = nc.dram_tensor("vg_d", [S, 65 * HEADS], BF16, addr_space="Shared")
    ef_d = nc.dram_tensor("ef_d", [ELEN + 1], F32)
    esh_d = nc.dram_tensor("esh_d", [128 * (ROWP + 1)], BF16)
    x1_d = nc.dram_tensor("x1_d", [T, H], F32)
    sc_d = nc.dram_tensor("sc_d", [4], F32)
    y67_d = nc.dram_tensor("y67_d", [256, H], F32)
    gbf_d = nc.dram_tensor("gbf_d", [IB * 128, H], BF16)
    ubf_d = nc.dram_tensor("ubf_d", [IB * 128, H], BF16)
    dbf_d = nc.dram_tensor("dbf_d", [IB * 128, H], BF16)
    wobf_d = nc.dram_tensor("wobf_d", [H, H], BF16)

    with ExitStack() as ctx:
        e = ctx.enter_context

        QT_C = max(FB * T, min(IB, 32) * 512, min(HC, 8) * H, WLEN + 1)
        CT_C = max(FB * T, max(0, IB - 32) * 512, WLEN + 1)
        XT_C = max(FB * T, 2 * KCQ * 260)
        KB_C = max(4 * SQ, 4 * H, 2 * H)
        WB_C = (2 * GD + 1) * H
        EX_C = max(4 * QF, H)
        AP_C = max(8 * QF, 65 * HEADS + T, 2 * H)

        QT = e(nc.sbuf_tensor("QT", [128, QT_C], BF16))
        CT = e(nc.sbuf_tensor("CT", [128, CT_C], BF16))
        XT = e(nc.sbuf_tensor("XT", [128, XT_C], BF16))
        KB = e(nc.sbuf_tensor("KB", [128, KB_C], BF16))
        WB = e(nc.sbuf_tensor("WB", [128, WB_C], BF16))
        EX = e(nc.sbuf_tensor("EX", [128, EX_C], BF16))
        APB = e(nc.sbuf_tensor("APB", [128, AP_C], BF16))
        ATB = e(nc.sbuf_tensor("ATB", [128, AP_C], BF16))
        XF = e(nc.sbuf_tensor("XF", [128, 2 * H], F32))
        FS = e(nc.sbuf_tensor("FS", [128, 4 * T], F32))
        FM = e(nc.sbuf_tensor("FM", [128, 2048], BF16))
        SID = e(nc.sbuf_tensor("SID", [128, 128], BF16))
        SAN = e(nc.sbuf_tensor("SAN", [128, 128], BF16))
        SSEL = e(nc.sbuf_tensor("SSEL", [8, 128], BF16))
        SREL = e(nc.sbuf_tensor("SREL", [128, 512], F32))
        SBV = e(nc.sbuf_tensor("SBV", [128, 8], F32))
        SBVB = e(nc.sbuf_tensor("SBVB", [128, 8], BF16))
        SGS = e(nc.sbuf_tensor("SGS", [128, IB], F32))
        SUS = e(nc.sbuf_tensor("SUS", [128, IB], F32))
        SCN = e(nc.sbuf_tensor("SCN", [128, 16], F32))
        ONEB = e(nc.sbuf_tensor("ONEB", [128, 1], BF16))
        ONEF = e(nc.sbuf_tensor("ONEF", [65, 64], F32))
        NEGR = ONEF.ap()[0:1, :].bitcast(BF16)

        P0 = e(nc.psum_tensor("P0", [128, 2048], F32))
        P1 = e(nc.psum_tensor("P1", [128, 2048], F32))

        qLD = Sem(ctx, nc, "qLD")
        qSA = Sem(ctx, nc, "qSA")
        qSB = Sem(ctx, nc, "qSB")
        qOUT = Sem(ctx, nc, "qOUT")
        qK_ = [[Sem(ctx, nc, f"qK{a}{b}") for b in range(2)] for a in range(2)]
        qV_ = [Sem(ctx, nc, f"qV{b}") for b in range(2)]
        qE_ = [Sem(ctx, nc, f"qE{b}") for b in range(4)]
        qGA = Sem(ctx, nc, "qGA")
        qGB = Sem(ctx, nc, "qGB")
        qMG = Sem(ctx, nc, "qMG")
        qD0 = Sem(ctx, nc, "qD0")
        qD1 = Sem(ctx, nc, "qD1")
        qPP = Sem(ctx, nc, "qPP")
        qWO = Sem(ctx, nc, "qWO")
        qP0 = Sem(ctx, nc, "qP0")
        qP1 = Sem(ctx, nc, "qP1")
        qP_ = [qP0, qP1]
        qD_ = [qD0, qD1]
        qS_ = [qSA, qSB]
        qG_ = [qGA, qGB]
        ePE = Sem(ctx, nc, "ePE")
        eACT = Sem(ctx, nc, "eACT")
        eDVE = Sem(ctx, nc, "eDVE")
        ePO = Sem(ctx, nc, "ePO")
        sCC = Sem(ctx, nc, "sCC")

        block = e(nc.Block())

        FSf = FS.ap()
        XFf = XF.ap()
        FMf = FM.ap()[:, 0:2048].bitcast(F32)
        P1b = P1.ap()[:, 0:512].bitcast(BF16)[:, 0:512]

        def xslot(i):
            return XFf[:, (i % 2) * H:((i % 2) * H + H)]

        def xlnslot(i):
            return APB.ap()[:, (i % 2) * H:((i % 2) * H + H)]

        def yacc(tb):
            if tb == 0:
                return APB.ap()[:, 0:2 * H].bitcast(F32)[:, 0:H]
            if tb == 1:
                return ATB.ap()[:, 0:2 * H].bitcast(F32)[:, 0:H]
            if tb in (2, 3):
                return XFf[:, (tb - 2) * H:((tb - 2) * H + H)]
            if tb in (4, 5):
                return KB.ap()[:, 0:4 * H].bitcast(F32)[:, (tb - 4) * H:(tb - 3) * H]
            return FSf[:, (tb - 6) * H:(tb - 5) * H]

        def hchunk(ib):
            if ib < 32:
                return QT.ap()[:, ib * 512:(ib + 1) * 512]
            return CT.ap()[:, (ib - 32) * 512:(ib - 31) * 512]

        def woslot(hc):
            if hc < min(HC, 8):
                return QT.ap()[:, hc * H:(hc + 1) * H]
            if hc < 12:
                return KB.ap()[:, (hc - 8) * H:(hc - 7) * H]
            return WB.ap()[:, (hc - 12) * H:(hc - 11) * H]

        def vchain(v, inst):
            eDVE.inc(inst)
            v.wait_ge(eDVE.h, eDVE.n)

        def newton_rsqrt(v):
            vv = SCN.ap()[:, 8:9]
            vh = SCN.ap()[:, 12:13]
            yy = SCN.ap()[:, 9:10]
            t1 = SCN.ap()[:, 13:14]
            vchain(v, v.tensor_scalar_mul(vh, vv, 0.5))
            b = vv.bitcast(I32)
            yb = yy.bitcast(I32)
            vchain(v, v.tensor_single_scalar(yb, b, 1, ALU.logical_shift_right))
            vchain(v, v.tensor_tensor(yb, SCN.ap()[:, 15:16].bitcast(I32), yb, ALU.subtract))
            for _ in range(3):
                vchain(v, v.tensor_tensor(t1, yy, yy, ALU.mult))
                vchain(v, v.tensor_tensor(t1, t1, vh, ALU.mult))
                vchain(v, v.tensor_scalar(t1, t1, -1.0, 1.5, ALU.mult, ALU.add))
                vchain(v, v.tensor_tensor(yy, yy, t1, ALU.mult))

        def emit_ln(v, src, dst_bf16, scratch=None):
            if scratch is None:
                scratch = P0.ap()[:, 0:H]
            mean = SCN.ap()[:, 0:1]
            ssq = SCN.ap()[:, 1:2]
            vchain(v, v.reduce_sum(mean, src, axis=AX.X))
            vchain(v, v.tensor_scalar_mul(mean, mean, 1.0 / H))
            vchain(v, v.scalar_tensor_tensor(scratch, src, mean, src,
                                             ALU.subtract, ALU.mult, accum_out=ssq))
            vchain(v, v.tensor_scalar(SCN.ap()[:, 8:9], ssq, 1.0 / H, EPS,
                                      ALU.mult, ALU.add))
            newton_rsqrt(v)
            vchain(v, v.tensor_scalar(dst_bf16, src, mean, SCN.ap()[:, 9:10],
                                      ALU.subtract, ALU.mult))

        @block.gpsimd
        def _(g):
            g.memset(SREL.ap(), 0.0)
            g.memset(ONEB.ap(), 1.0)
            g.memset(NEGR, -128.0)
            ePO.inc(g.memset(ONEF.ap()[64:65, :], 1.0))
        po_consts = ePO.n

        @block.sync
        def _(sy):
            qLD.dma(sy.dma_start(out=SID.ap(), in_=ident_in.ap()))
            qLD.dma(sy.dma_start(out=SAN.ap(), in_=anti_in.ap()))
            qLD.dma(sy.dma_start(out=SSEL.ap(), in_=sel_in.ap()))
            qLD.dma(sy.dma_start(out=SGS.ap(), in_=gate_s.ap()))
            qLD.dma(sy.dma_start(out=SUS.ap(), in_=up_s.ap()))
            sy.wait_ge(ePO.h, po_consts)
            qLD.dma(sy.dma_start(
                out=SREL.ap()[:, 0:448].rearrange("p (g d) -> p g d", g=7),
                in_=rel.ap()[0:896, :].rearrange("(g p) d -> p g d", p=128)))
            qLD.dma(sy.dma_start(out=SREL.ap()[0:127, 448:512],
                                 in_=rel.ap()[896:1023, :]))
        ld_setup = qLD.n

        @block.vector
        def _(v):
            v.wait_ge(qLD.h, ld_setup)
            for g8 in range(8):
                eDVE.inc(v.reduce_sum(SBV.ap()[:, g8:g8 + 1],
                                      SREL.ap()[:, g8 * 64:(g8 + 1) * 64], axis=AX.X))
            v.wait_ge(eDVE.h, eDVE.n)
            vchain(v, v.tensor_scalar_mul(SBV.ap(), SBV.ap(), 1.0 / 64.0))
            vchain(v, v.tensor_copy(SBVB.ap(), SBV.ap()))
            vchain(v, v.memset(SCN.ap()[:, 15:16].bitcast(I32), 0x5F3759DF))
        dv_bv = eDVE.n

        @block.tensor
        def _(t):
            t.wait_ge(eDVE.h, dv_bv)
            for g8 in range(8):
                inst = t.matmul(P1.ap()[0:1, (7 - g8) * 128:(8 - g8) * 128],
                                SBVB.ap()[:, g8:g8 + 1], SAN.ap(),
                                start=True, stop=True)
            ePE.inc(inst)
        pe_rev = ePE.n

        @block.scalar
        def _(s):
            s.wait_ge(ePE.h, pe_rev)
            eACT.inc(s.copy(FSf[0:1, 0:1024], P1.ap()[0:1, 0:1024]))
        act_rev = eACT.n

        RE = (ELEN + 1) // 1024
        r1, c1 = divmod(S - 512, 1024)

        @block.sync
        def _(sy):
            sy.wait_ge(eACT.h, act_rev)
            qLD.dma(sy.dma_start(out=sc_d.ap()[0:1], in_=FSf[0:1, 1:2],
                                 allow_slow_non_contiguous=True))
            qLD.dma(sy.dma_start(out=sc_d.ap()[1:2], in_=FSf[0:1, 1023:1024],
                                 allow_slow_non_contiguous=True))
            sy.wait_ge(qLD.h, qLD.n)
            qLD.dma(sy.dma_start(out=SCN.ap()[0:r1 + 1, 10:11],
                                 in_=bass.AP(sc_d, 0, [[0, r1 + 1], [1, 1]]),
                                 allow_slow_non_contiguous=True))
            qLD.dma(sy.dma_start(out=SCN.ap()[r1 + 1:RE, 10:11],
                                 in_=bass.AP(sc_d, 1, [[0, RE - (r1 + 1)], [1, 1]]),
                                 allow_slow_non_contiguous=True))
        ld_scol = qLD.n

        @block.vector
        def _(v):
            v.wait_ge(qLD.h, ld_scol)
            SEF2 = XFf[0:RE, 0:1024]
            vchain(v, v.memset(SEF2, 0.0))
            vchain(v, v.tensor_scalar_add(SEF2, SEF2, SCN.ap()[0:RE, 10:11]))
        dv_sef = eDVE.n

        @block.sync
        def _(sy):
            sy.wait_ge(eDVE.h, dv_sef)
            pos = 0
            rr, cc = r1, c1
            while pos < 1023:
                seg = min(1023 - pos, 1024 - cc)
                qLD.dma(sy.dma_start(out=XFf[rr:rr + 1, cc:cc + seg],
                                     in_=FSf[0:1, 1 + pos:1 + pos + seg]))
                pos += seg
                rr += 1
                cc = 0
            sy.wait_ge(qLD.h, qLD.n)
            qLD.dma(sy.dma_start(
                out=ef_d.ap().rearrange("(r c) -> r c", c=1024),
                in_=XFf[0:RE, 0:1024]))
        ld_ef = qLD.n

        @block.gpsimd
        def _(g):
            g.wait_ge(qLD.h, ld_ef)
            qMG.dma(g.dma_start(out=QT.ap()[0:8, 0:WLEN],
                                in_=bass.AP(ef_d, 0, [[T, 8], [1, WLEN]])))
        lg_cand = qMG.n

        ppin_ev = {}
        pp_early = []
        for i_ in range(min(8, IB // 2)):
            pp_early += [i_, IB + i_]
        pp_early_set = set(pp_early)
        pp_order = pp_early + [k for k in range(3 * IB) if k not in pp_early_set]
        pp_seg = pp_order

        pe_w0 = ePE.n
        act_w0 = eACT.n

        @block.tensor
        def _(t):
            t.wait_ge(qMG.h, lg_cand)
            t.wait_ge(eACT.h, act_rev)
            for w in range(NW):
                lo = w * 512
                ln = min(512, WLEN - lo)
                if w >= 2:
                    t.wait_ge(eACT.h, act_w0 + (w - 2) + 1)
                ePE.inc(t.matmul(P1.ap()[:, (w % 2) * 512:(w % 2) * 512 + ln],
                                 SSEL.ap(), QT.ap()[0:8, lo:lo + ln],
                                 start=True, stop=True))

        @block.scalar
        def _(s):
            for w in range(NW):
                lo = w * 512
                ln = min(512, WLEN - lo)
                s.wait_ge(ePE.h, pe_w0 + w + 1)
                eACT.inc(s.activation(CT.ap()[:, lo:lo + ln],
                                      P1.ap()[:, (w % 2) * 512:(w % 2) * 512 + ln],
                                      AF.Exp))
        act_wexp = eACT.n

        @block.sync
        def _(sy):
            sy.wait_ge(eACT.h, act_wexp)
            qLD.dma(sy.dma_start(out=bass.AP(esh_d, 0, [[ROWP + 1, 128], [1, WLEN]]),
                                 in_=CT.ap()[:, 0:WLEN]))
        ld_esh = qLD.n

        @block.gpsimd
        def _(g):
            g.wait_ge(qLD.h, ld_esh)
            for hc in range(HC):
                qP0.dma(g.dma_start(
                    out=wobf_d.ap()[hc * 128:(hc + 1) * 128, :],
                    in_=wo_t2.ap()[hc * 128:(hc + 1) * 128, :]))
        wocast_tot = qP0.n

        def emit_transposes(src_tile_fn, dst_col_fn, first_pe_waits):
            copies_base = eDVE.n
            pe_list = []

            @block.tensor
            def _(t_):
                for (sem, val) in first_pe_waits:
                    t_.wait_ge(sem, val)
                for b in range(NB4):
                    if b >= 1:
                        t_.wait_ge(eDVE.h, copies_base + b)
                    for j in range(4):
                        inst = t_.transpose(P1b[:, j * 128:(j + 1) * 128],
                                            src_tile_fn(b * 4 + j), SID.ap())
                    pe_list.append(ePE.inc(inst))

            @block.vector
            def _(v):
                for b in range(NB4):
                    v.wait_ge(ePE.h, pe_list[b])
                    eDVE.inc(v.tensor_copy(
                        dst_col_fn(b),
                        P1b.rearrange("p (f u) -> p f u", f=4)))
            assert eDVE.n == copies_base + NB4
            return pe_list[-1], eDVE.n

        ln_ev = {}
        tr_pe_ev = {}
        x_ev = {}
        ptcopy_last = 0
        for tt in range(TT):
            @block.sync
            def _(sy, tt=tt):
                if tt >= 2:
                    sy.wait_ge(eDVE.h, ln_ev[tt - 2])
                x_ev[tt] = qS_[tt % 2].dma(sy.dma_start(
                    out=xslot(tt), in_=x_in.ap()[tt * 128:(tt + 1) * 128, :]))

            @block.vector
            def _(v, tt=tt):
                v.wait_ge(qS_[tt % 2].h, x_ev[tt])
                if tt >= 2:
                    v.wait_ge(ePE.h, tr_pe_ev[tt - 2])
                emit_ln(v, xslot(tt), xlnslot(tt))
                ln_ev[tt] = eDVE.inc(v.engine_nop())

            waits = [(eDVE.h, ln_ev[tt])]
            if tt == 0:
                waits.append((qLD.h, ld_esh))
            else:
                waits.append((eDVE.h, ptcopy_last))

            def mksrc(tt):
                return lambda fb: xlnslot(tt)[:, fb * 128:(fb + 1) * 128]

            def mkdst(tt):
                return lambda b: (XT.ap()[:, 0:FB * T]
                                  .rearrange("p (f u) -> p f u", f=FB)
                                  [:, b * 4:b * 4 + 4, tt * 128:(tt + 1) * 128])

            tr_pe_ev[tt], ptcopy_last = emit_transposes(mksrc(tt), mkdst(tt), waits)
        ph1_end_dve = eDVE.n

        def emit_proj(wt, evac, first_waits=(), prev_mm=0):
            mm_ev = {}
            wl_ev = {}
            ev_done = {}

            def emit_load_cast(fb):
                @block.gpsimd
                def _(g, fb=fb):
                    if fb >= 2:
                        g.wait_ge(ePE.h, mm_ev[fb - 2])
                    elif prev_mm:
                        g.wait_ge(ePE.h, prev_mm)
                    wl_ev[fb] = qD_[fb % 2].dma(g.dma_start(
                        out=WB.ap()[:, (fb % 2) * H:((fb % 2) * H + H)],
                        in_=wt.ap()[fb]))

            def emit_mm(fb):
                @block.tensor
                def _(t_, fb=fb):
                    t_.wait_ge(qD_[fb % 2].h, wl_ev[fb])
                    if fb == 0:
                        for (semh, val) in first_waits:
                            t_.wait_ge(semh, val)
                    if fb >= 1:
                        t_.wait_ge(eACT.h, ev_done[fb - 1])
                    for hc in range(HC):
                        for th2 in range(NTH2):
                            inst = t_.matmul(
                                P0.ap()[:, th2 * 512:(th2 + 1) * 512],
                                WB.ap()[:, (fb % 2) * H + hc * 128:(fb % 2) * H + (hc + 1) * 128],
                                XT.ap()[:, hc * T + th2 * 512:hc * T + th2 * 512 + 512],
                                start=(hc == 0), stop=(hc == HC - 1))
                    mm_ev[fb] = ePE.inc(inst)

            emit_load_cast(0)
            emit_mm(0)
            for fb in range(1, FB):
                emit_load_cast(fb)
                evac(fb - 1, mm_ev[fb - 1])
                ev_done[fb - 1] = eACT.n
                emit_mm(fb)
            evac(FB - 1, mm_ev[FB - 1])
            ev_done[FB - 1] = eACT.n
            return mm_ev

        kst_ev = {}

        def kstg(fb):
            base = 65 * HEADS
            return (APB.ap()[:, base:base + T] if fb % 2 == 0
                    else ATB.ap()[:, base:base + T])

        def kevac(fb, mmv):
            @block.scalar
            def _(s, fb=fb, mmv=mmv):
                s.wait_ge(ePE.h, mmv)
                if fb >= 2:
                    s.wait_ge(qS_[fb % 2].h, kst_ev[fb - 2])
                eACT.inc(s.copy(kstg(fb), P0.ap()[:, 0:NTH2 * 512]))
            a_now = eACT.n

            @block.sync
            def _(sy, fb=fb, a_now=a_now):
                sy.wait_ge(eACT.h, a_now)
                kst_ev[fb] = qS_[fb % 2].dma(sy.dma_start(
                    out=kc_d.ap()[fb * 128:(fb + 1) * 128, :], in_=kstg(fb)))

        def vevac(fb, mmv):
            @block.scalar
            def _(s, fb=fb, mmv=mmv):
                s.wait_ge(ePE.h, mmv)
                if fb == 0:
                    s.wait_ge(qLD.h, ld_esh)
                eACT.inc(s.copy(CT.ap()[:, fb * T:fb * T + T], P0.ap()[:, 0:NTH2 * 512]))

        def qevac(fb, mmv):
            @block.scalar
            def _(s, fb=fb, mmv=mmv):
                s.wait_ge(ePE.h, mmv)
                eACT.inc(s.activation(QT.ap()[:, fb * T:fb * T + T],
                                      P0.ap()[:, 0:NTH2 * 512], AF.Copy, scale=0.125))

        k_mm = emit_proj(wk_t, kevac, first_waits=[(eDVE.h, ph1_end_dve)])

        kst_totA, kst_totB = qSA.n, qSB.n

        @block.gpsimd
        def _(g):
            g.wait_ge(qSA.h, kst_totA)
            g.wait_ge(qSB.h, kst_totB)
            sCC.inc(g.collective_compute(
                "AllGather", ALU.bypass, replica_groups=[list(range(NCORES))],
                ins=[kc_d.ap()], outs=[kg_d.ap()]))

        v_mm = emit_proj(wv_t, vevac, first_waits=[(eACT.h, eACT.n)],
                         prev_mm=k_mm[FB - 1])
        v_all = eACT.n

        vst_ev = {}
        vtp_last = ptcopy_last
        for tb in range(TT):
            vcb = (APB.ap()[:, 0:65 * HEADS] if tb % 2 == 0
                   else ATB.ap()[:, 0:65 * HEADS])

            @block.gpsimd
            def _(g, tb=tb, vcb=vcb):
                if tb >= 2:
                    g.wait_ge(qS_[tb % 2].h, vst_ev[tb - 2])
                ePO.inc(g.memset(vcb, 1.0))
            po_now = ePO.n

            waits = [(eACT.h, v_all), (ePO.h, po_now), (eDVE.h, vtp_last)]

            def mksrcv(tb):
                return lambda fb: CT.ap()[:, fb * T + tb * 128:fb * T + tb * 128 + 128]

            def mkdstv(vcb):
                return lambda b: (vcb.rearrange("p (f u) -> p f u", f=HEADS // 2)
                                  [:, b * 4:b * 4 + 4, :]
                                  .rearrange("p f (g u) -> p f g u", g=2)[:, :, :, 0:64])

            copies_base = eDVE.n
            pe_list = []

            @block.tensor
            def _(t_, tb=tb):
                for (semh, val) in waits:
                    t_.wait_ge(semh, val)
                for b in range(NB4):
                    if b >= 1:
                        t_.wait_ge(eDVE.h, copies_base + b)
                    for j in range(4):
                        inst = t_.transpose(P1b[:, j * 128:(j + 1) * 128],
                                            mksrcv(tb)(b * 4 + j), SID.ap())
                    pe_list.append(ePE.inc(inst))

            @block.vector
            def _(v, tb=tb, vcb=vcb):
                for b in range(NB4):
                    v.wait_ge(ePE.h, pe_list[b])
                    eDVE.inc(v.tensor_copy(
                        mkdstv(vcb)(b),
                        P1b[:, 0:512].rearrange("p (f g u) -> p f g u", f=4, g=2)))
            vtp_last = eDVE.n

            @block.sync
            def _(sy, tb=tb, vcb=vcb, d_now=vtp_last):
                sy.wait_ge(eDVE.h, d_now)
                vst_ev[tb] = qS_[tb % 2].dma(sy.dma_start(
                    out=vc_d.ap()[tb * 128:(tb + 1) * 128, :], in_=vcb))

        vst_totA, vst_totB = qSA.n, qSB.n

        @block.gpsimd
        def _(g):
            g.wait_ge(qSA.h, vst_totA)
            g.wait_ge(qSB.h, vst_totB)
            sCC.inc(g.collective_compute(
                "AllGather", ALU.bypass, replica_groups=[list(range(NCORES))],
                ins=[vc_d.ap()], outs=[vg_d.ap()]))
        cc_all = sCC.n

        emit_proj(wq_t, qevac, first_waits=[(eACT.h, v_all)],
                  prev_mm=v_mm[FB - 1])
        q_mm_last = ePE.n
        q_all = eACT.n

        s01_evs, s23_evs, av_evs = {}, {}, {}
        eA_evs, eB_evs, m01_evs, m23_evs, eb_evs = {}, {}, {}, {}, {}
        woS_ev = {}

        @block.sync
        def _(sy):
            sy.wait_ge(ePE.h, q_mm_last)
            sy.wait_ge(qP0.h, wocast_tot)
            for hc in range(12, min(HC, 16)):
                woS_ev[hc] = qWO.dma(sy.dma_start(
                    out=woslot(hc), in_=wobf_d.ap()[hc * 128:(hc + 1) * 128, :]))
        NSEG = QH * NG
        SPB = -(-len(pp_seg) // NSEG) if pp_seg else 1
        ctxfree = vtp_last
        p0free = (eACT.h, q_all)
        tg = 0

        for qh in range(QH):
            for grp in range(NG):
                seg0 = tg
                pe_base, act_base, dve_base = ePE.n, eACT.n, eDVE.n
                for kc in range(KC):
                    t = seg0 + kc
                    if kc == 0:
                        s01_evs[t] = pe_base + 1
                        s23_evs[t] = pe_base + 2
                    else:
                        s01_evs[t] = pe_base + 3 * kc
                        s23_evs[t] = pe_base + 3 * kc + 1
                    av_evs[t] = pe_base + (3 * kc + 5 if kc < KC - 1 else 3 * KC)
                    eA_evs[t] = act_base + 2 * kc + 1
                    eB_evs[t] = act_base + 2 * kc + 2
                    m01_evs[t] = dve_base + 2 * kc + 1
                    m23_evs[t] = dve_base + 2 * kc + 2

                kq_ev = {}
                vq_ev = {}

                @block.sync
                def _(sy, grp=grp, seg0=seg0):
                    if seg0 == 0:
                        sy.wait_ge(sCC.h, cc_all)
                        sy.wait_ge(ePE.h, q_mm_last)
                    for q in range(4):
                        if q >= 2:
                            sy.wait_ge(ePE.h, s23_evs[seg0 + (q - 1) * KCQ - 1])
                        elif seg0 > 0:
                            sy.wait_ge(ePE.h, s23_evs[seg0 - 1 - (1 - q) * KCQ])
                        for pi in range(2):
                            pp = 2 * grp + pi
                            kq_ev[(pi, q)] = qK_[pi][q % 2].dma(sy.dma_start(
                                out=KB.ap()[:, (pi * 2 + q % 2) * SQ:(pi * 2 + q % 2) * SQ + SQ]
                                    .rearrange("p (r u) -> p r u", r=2),
                                in_=kg_d.ap().rearrange("(r h) u -> r h u", h=H)
                                    [2 * q:2 * q + 2, pp * 128:(pp + 1) * 128, :]
                                    .transpose([1, 0, 2])))
                        if q >= 2:
                            sy.wait_ge(ePE.h, av_evs[seg0 + (q - 1) * KCQ - 1])
                        elif seg0 > 0:
                            sy.wait_ge(ePE.h, av_evs[seg0 - 1 - (1 - q) * KCQ])
                        vq_ev[q] = qV_[q % 2].dma(sy.dma_start(
                            out=XT.ap()[:, (q % 2) * KCQ * 260:((q % 2) + 1) * KCQ * 260]
                                .rearrange("p (k u) -> p k u", k=KCQ),
                            in_=vg_d.ap().rearrange("(k p) u -> p k u", p=128)
                                [:, q * KCQ:(q + 1) * KCQ, grp * 260:(grp + 1) * 260]))

                si = seg0 // KC
                myk = pp_seg[si * SPB:(si + 1) * SPB]
                kq_sched = {}
                for xi, k in enumerate(myk):
                    kq_sched[min(KC - 1, 2 + xi * max(1, KC // max(1, len(myk))))] = k
                if si == NSEG - 1 and HC > 7 and seg0 > 0:
                    @block.sync
                    def _(sy, wo_prev=s23_evs[seg0 - 1]):
                        sy.wait_ge(ePE.h, wo_prev)
                        for hc in range(0, 7):
                            woS_ev[hc] = qWO.dma(sy.dma_start(
                                out=woslot(hc),
                                in_=wobf_d.ap()[hc * 128:(hc + 1) * 128, :]))

                @block.gpsimd
                def _(g, qh=qh, seg0=seg0, myk=myk, kq_sched=kq_sched):
                    for kc in range(KC):
                        t = seg0 + kc
                        if t == 0:
                            g.wait_ge(qLD.h, ld_esh)
                        if t >= 4:
                            g.wait_ge(eDVE.h, m23_evs[t - 4])
                        off = (S - 1) + 512 * qh - 128 * kc
                        eb_evs[t] = qE_[kc % 4].dma(g.dma_start(
                            out=EX.ap()[:, (kc % 4) * 512:(kc % 4) * 512 + 512],
                            in_=bass.AP(esh_d, off, [[ROWP, 128], [1, 512]])))
                        if kc in kq_sched:
                            k = kq_sched[kc]
                            srcs = (gate_t, up_t, down_t)
                            dsts = (gbf_d, ubf_d, dbf_d)
                            r = k % IB
                            csem = qPP if k in pp_early_set else qG_[k % 2]
                            ppin_ev[k] = csem.dma(g.dma_start(
                                out=dsts[k // IB].ap()[r * 128:(r + 1) * 128, :],
                                in_=srcs[k // IB].ap()[k % IB]))

                @block.tensor
                def _(t_, qh=qh, grp=grp, seg0=seg0, p0free=p0free, ctxfree=ctxfree):
                    for kc in range(KC):
                        t = seg0 + kc
                        q = kc // KCQ
                        kq = kc % KCQ
                        if kq == 0:
                            for pi in range(2):
                                t_.wait_ge(qK_[pi][q % 2].h, kq_ev[(pi, q)])
                        for half2 in range(2):
                            if kc == 0:
                                if half2 == 0:
                                    t_.wait_ge(*p0free)
                            else:
                                t_.wait_ge(eACT.h, (eA_evs if half2 == 0 else eB_evs)[t - 1])
                            for j in (0, 1) if half2 == 0 else (2, 3):
                                pi, sg = j // 2, j % 2
                                ks = (pi * 2 + q % 2) * SQ
                                inst = t_.matmul(
                                    P0.ap()[:, j * 512:(j + 1) * 512],
                                    KB.ap()[sg * 64:(sg + 1) * 64,
                                            ks + kq * 128:ks + (kq + 1) * 128],
                                    QT.ap()[sg * 64:(sg + 1) * 64,
                                            (2 * grp + pi) * T + qh * 512:(2 * grp + pi) * T + qh * 512 + 512],
                                    start=True, stop=True, tile_position=(sg * 64, 0))
                            assert ePE.inc(inst) == (s01_evs if half2 == 0 else s23_evs)[t]
                        kcps = [kc - 1, kc] if kc == KC - 1 else ([kc - 1] if kc >= 1 else [])
                        for kcp in kcps:
                            tp = seg0 + kcp
                            qp = kcp // KCQ
                            kqp = kcp % KCQ
                            if kqp == 0:
                                t_.wait_ge(qV_[qp % 2].h, vq_ev[qp])
                            if kcp == 0:
                                t_.wait_ge(eDVE.h, ctxfree)
                            for j in range(4):
                                if j == 0:
                                    t_.wait_ge(eDVE.h, m01_evs[tp])
                                if j == 2:
                                    t_.wait_ge(eDVE.h, m23_evs[tp])
                                vb = (qp % 2) * KCQ * 260 + kqp * 260 + j * 65
                                inst = t_.matmul(
                                    P1.ap()[0:65, j * 512:(j + 1) * 512],
                                    XT.ap()[:, vb:vb + 65],
                                    ATB.ap()[:, (tp % 2) * 2048 + j * 512:(tp % 2) * 2048 + (j + 1) * 512],
                                    start=(kcp == 0), stop=(kcp == KC - 1))
                            assert ePE.inc(inst) == av_evs[tp]

                @block.scalar
                def _(s, seg0=seg0):
                    for kc in range(KC):
                        t = seg0 + kc
                        s.wait_ge(ePE.h, s01_evs[t])
                        if t >= 2:
                            s.wait_ge(eDVE.h, m01_evs[t - 2])
                        assert eACT.inc(s.activation(
                            APB.ap()[:, (t % 2) * 2048:(t % 2) * 2048 + 1024],
                            P0.ap()[:, 0:1024], AF.Exp)) == eA_evs[t]
                        s.wait_ge(ePE.h, s23_evs[t])
                        if t >= 2:
                            s.wait_ge(eDVE.h, m23_evs[t - 2])
                        assert eACT.inc(s.activation(
                            APB.ap()[:, (t % 2) * 2048 + 1024:(t % 2) * 2048 + 2048],
                            P0.ap()[:, 1024:2048], AF.Exp)) == eB_evs[t]

                @block.vector
                def _(v, seg0=seg0):
                    for kc in range(KC):
                        t = seg0 + kc
                        v.wait_ge(qE_[kc % 4].h, eb_evs[t])
                        ebrep = bass.AP(EX.ap().tensor, (kc % 4) * 512,
                                        [[EX_C, 128], [0, 2], [1, 512]])
                        for half2 in range(2):
                            v.wait_ge(eACT.h, (eA_evs if half2 == 0 else eB_evs)[t])
                            if t >= 2:
                                v.wait_ge(ePE.h, av_evs[t - 2])
                            base = (t % 2) * 2048 + half2 * 1024
                            inst = v.tensor_tensor(
                                ATB.ap()[:, base:base + 1024].rearrange("p (r u) -> p r u", r=2),
                                APB.ap()[:, base:base + 1024].rearrange("p (r u) -> p r u", r=2),
                                ebrep, ALU.mult)
                            assert eDVE.inc(inst) == (m01_evs if half2 == 0 else m23_evs)[t]

                tlast = seg0 + KC - 1

                rec_base = eDVE.n

                @block.vector
                def _(v, tlast=tlast):
                    v.wait_ge(ePE.h, av_evs[tlast])
                    for j in range(4):
                        eDVE.inc(v.reciprocal(FSf[64:65, j * 512:(j + 1) * 512],
                                              P1.ap()[64:65, j * 512:(j + 1) * 512]))

                @block.tensor
                def _(t_, rb=rec_base, tlast=tlast):
                    t_.wait_ge(eACT.h, eB_evs[tlast])
                    for j in range(4):
                        t_.wait_ge(eDVE.h, rb + j + 1)
                        inst = t_.matmul(P0.ap()[0:64, j * 512:(j + 1) * 512],
                                         ONEF.ap()[64:65, :],
                                         FSf[64:65, j * 512:(j + 1) * 512],
                                         start=True, stop=True, tile_position=(64, 0))
                        ePE.inc(inst)
                bc_ev = ePE.n
                assert bc_ev == pe_base + 3 * KC + 4

                @block.vector
                def _(v, bc_ev=bc_ev, qh=qh, grp=grp):
                    v.wait_ge(ePE.h, bc_ev)
                    vchain(v, v.tensor_copy(FSf[0:64, 0:2048], P0.ap()[0:64, :]))
                    for j in range(4):
                        pi, sg = j // 2, j % 2
                        dst = CT.ap()[sg * 64:(sg + 1) * 64,
                                      (2 * grp + pi) * T + qh * 512:(2 * grp + pi) * T + qh * 512 + 512]
                        inst = v.tensor_tensor(dst, P1.ap()[0:64, j * 512:(j + 1) * 512],
                                               FSf[0:64, j * 512:(j + 1) * 512], ALU.mult)
                    eDVE.inc(inst)
                ctxfree = eDVE.n
                assert ctxfree == dve_base + 2 * KC + 6
                p0free = (eDVE.h, dve_base + 2 * KC + 5)
                tg += KC

        assert len(ppin_ev) == 3 * IB, (len(ppin_ev), 3 * IB)
        pp_totA, pp_totB = qGA.n, qGB.n
        qpp_tot = qPP.n
        last_s = s23_evs[tg - 1]

        wo_ev = {}

        @block.sync
        def _(sy):
            sy.wait_ge(ePE.h, last_s)
            sy.wait_ge(qP0.h, wocast_tot)
            order = [h for h in (8, 9, 10, 11, 7) if h < HC] + \
                    [h for h in range(HC) if h not in (7, 8, 9, 10, 11)]
            for hc in order:
                if hc in woS_ev:
                    continue
                woS_ev[hc] = qWO.dma(sy.dma_start(
                    out=woslot(hc), in_=wobf_d.ap()[hc * 128:(hc + 1) * 128, :]))

        op_ev = {}
        x1_ev = {}
        x1add_ev = {}
        x1st_ev = {}
        xr_ev = {}
        x1t_pe_ev = {}
        ptcopy_last4 = None

        def mksrc4(tb):
            return lambda fb: xlnslot(tb)[:, fb * 128:(fb + 1) * 128]

        def mkdst4(tb):
            return lambda b: (XT.ap()[:, 0:FB * T]
                              .rearrange("p (f u) -> p f u", f=FB)
                              [:, b * 4:b * 4 + 4, tb * 128:(tb + 1) * 128])

        def emit_x1t(tp):
            waits = [(eDVE.h, x1_ev[tp])]
            if tp == 0:
                waits.append((eDVE.h, ctxfree))
            else:
                waits.append((eDVE.h, ptcopy_last4))
            return emit_transposes(mksrc4(tp), mkdst4(tp), waits)

        for tb in range(TT):
            @block.sync
            def _(sy, tb=tb):
                if tb == 0:
                    sy.wait_ge(qSA.h, qSA.n)
                    sy.wait_ge(qSB.h, qSB.n)
                if tb >= 2:
                    sy.wait_ge(eDVE.h, x1_ev[tb - 2])
                xr_ev[tb] = qS_[tb % 2].dma(sy.dma_start(
                    out=xslot(tb), in_=x_in.ap()[tb * 128:(tb + 1) * 128, :]))

            @block.tensor
            def _(t_, tb=tb):
                if tb == 0:
                    t_.wait_ge(*p0free)
                    t_.wait_ge(eDVE.h, ctxfree)
                else:
                    t_.wait_ge(eDVE.h, x1add_ev[tb - 1])
                if HC > 12:
                    hc_order = [12, 13, 14, 15] + list(range(0, 7)) + [8, 9, 10, 11, 7]
                else:
                    hc_order = list(range(HC))
                for oi, hc in enumerate(hc_order):
                    if tb == 0 and oi == 0:
                        w0 = max(woS_ev[h2] for h2 in
                                 list(range(min(7, HC))) + [h for h in (12, 13, 14, 15) if h < HC])
                        t_.wait_ge(qWO.h, w0)
                    if tb == 0 and HC > 12 and hc in (7, 8, 9, 10, 11):
                        t_.wait_ge(qWO.h, woS_ev[hc])
                    for fq in range(FQN):
                        inst = t_.matmul(
                            P0.ap()[:, fq * 512:(fq + 1) * 512],
                            CT.ap()[:, hc * T + tb * 128:hc * T + tb * 128 + 128],
                            woslot(hc)[:, fq * 512:(fq + 1) * 512],
                            start=(oi == 0), stop=(oi == HC - 1))
                op_ev[tb] = ePE.inc(inst)

            @block.vector
            def _(v, tb=tb):
                v.wait_ge(ePE.h, op_ev[tb])
                v.wait_ge(qS_[tb % 2].h, xr_ev[tb])
                if tb >= 1:
                    v.wait_ge(qP1.h, x1st_ev[tb - 1])
                vchain(v, v.tensor_tensor(FSf[:, 0:H], P0.ap()[:, 0:H],
                                          xslot(tb), ALU.add))
                x1add_ev[tb] = eDVE.n

            if tb >= 1:
                x1t_pe_ev[tb - 1], ptcopy_last4 = emit_x1t(tb - 1)

            @block.vector
            def _(v, tb=tb):
                if tb >= 2:
                    v.wait_ge(ePE.h, x1t_pe_ev[tb - 2])
                emit_ln(v, FSf[:, 0:H], xlnslot(tb), scratch=FSf[:, H:2 * H])
                x1_ev[tb] = eDVE.inc(v.engine_nop())

            @block.scalar
            def _(s, tb=tb):
                s.wait_ge(eDVE.h, x1add_ev[tb])
                x1st_ev[tb] = qP1.dma(s.dma_start(
                    out=x1_d.ap()[tb * 128:(tb + 1) * 128, :], in_=FSf[:, 0:H]))

        x1t_pe_ev[TT - 1], ptcopy_last4 = emit_x1t(TT - 1)
        ph4_end_dve = eDVE.n
        ph4_end_pe = ePE.n
        x1st_fence = [(qP1, x1st_ev[TT - 1])]

        IBH = IB // 2
        NTF = T // 512
        assert IB % 2 == 0
        QTH = QT_C // T

        def hslot(j):
            if j < QTH:
                return QT.ap()[:, j * T:(j + 1) * T]
            return CT.ap()[:, (j - QTH) * T:(j - QTH + 1) * T]

        def guw(j, up):
            s_ = ((6 if up else 4) + j) if j < 2 else ((2 if up else 0) + j % 2)
            return WB.ap()[:, s_ * H:(s_ + 1) * H]

        def dwslot(j):
            s_ = (j + 4) % 8
            return WB.ap()[:, s_ * H:(s_ + 1) * H]

        SDSB = WB.ap()[:, 2 * GD * H:2 * GD * H + H]
        yd_sems = [qK_[0][0], qK_[0][1], qK_[1][0], qK_[1][1],
                   qV_[0], qV_[1], qE_[0], qE_[1]]

        @block.gpsimd
        def _(g):
            g.wait_ge(ePE.h, ph4_end_pe)
            qMG.dma(g.dma_start(out=SDSB,
                                in_=bass.AP(down_s, 0, [[0, 128], [1, H]])))
        sdsb_ev = qMG.n

        @block.tensor
        def _(t_):
            t_.wait_ge(eDVE.h, ph4_end_dve)
            for tf in range(NTF):
                for hc in range(HC):
                    inst = t_.matmul(P0.ap()[0:1, tf * 512:tf * 512 + 512], ONEB.ap(),
                                     XT.ap()[:, hc * T + tf * 512:hc * T + tf * 512 + 512],
                                     start=(hc == 0), stop=(hc == HC - 1))
            ePE.inc(inst)
        rsx_pe = ePE.n

        @block.vector
        def _(v):
            v.wait_ge(ePE.h, rsx_pe)
            for (sem_, val_) in x1st_fence:
                v.wait_ge(sem_.h, val_)
            vchain(v, v.tensor_copy(FSf[0:1, 0:T], P0.ap()[0:1, 0:T]))
            vchain(v, v.tensor_copy(FM.ap()[0:1, 0:T], FSf[0:1, 0:T]))
            vchain(v, v.tensor_sub(FM.ap()[0:1, T:2 * T], FSf[0:1, 0:T],
                                   FM.ap()[0:1, 0:T]))
        rsx_dve = eDVE.n

        prev_half_pe = ph4_end_pe
        prev_half_dve = eDVE.n
        prev_half_d6 = 0
        spill_ev = 0
        yadd_ev = {}
        dsems = [qGB, qD0, qD1, qMG]
        for half in range(2):
            draw_ev = {}
            scale_ev = {}
            gul_ev = {}
            gmm_ev = {}
            gu_act = {}
            gu_dve = {}
            gu_ev = {}
            h_ev = {}
            for j in range(IBH):
                ib = half * IBH + j

                @block.sync
                def _(sy, ib=ib, j=j, ph6=prev_half_d6, half=half):
                    if j >= 2:
                        sy.wait_ge(ePE.h, gmm_ev[j - 2])
                    else:
                        sy.wait_ge(qV_[j % 2].h, qV_[j % 2].n)
                        if half == 1:
                            sy.wait_ge(ePE.h, ph6)
                    if half == 0 and j < 8:
                        sy.wait_ge(qPP.h, qpp_tot)
                    elif (half == 0 and j in (8, 9)) or (half == 1 and j < 2):
                        sy.wait_ge(qGA.h, pp_totA)
                        sy.wait_ge(qGB.h, pp_totB)
                    qV_[j % 2].dma(sy.dma_start(
                        out=guw(j, False),
                        in_=gbf_d.ap()[ib * 128:(ib + 1) * 128, :]))
                    gul_ev[j] = qV_[j % 2].dma(sy.dma_start(
                        out=guw(j, True),
                        in_=ubf_d.ap()[ib * 128:(ib + 1) * 128, :]))

                @block.tensor
                def _(t_, ib=ib, j=j, half=half, phd=prev_half_dve):
                    t_.wait_ge(qV_[j % 2].h, gul_ev[j])
                    if j == 0:
                        t_.wait_ge(eDVE.h, rsx_dve)
                    if half == 1 and j < 2:
                        t_.wait_ge(eDVE.h, phd)
                    if j >= 2:
                        t_.wait_ge(eACT.h, gu_act[j - 2])
                        t_.wait_ge(eDVE.h, gu_dve[j - 2])
                        t_.wait_ge(ePO.h, gu_ev[j - 2])
                    PS = P0.ap() if j % 2 == 0 else P1.ap()
                    for (base, up_) in ((0, False), (T, True)):
                        w_ap = guw(j, up_)
                        for hc in range(HC):
                            for tf in range(NTF):
                                t_.matmul(
                                    PS[:, base + tf * 512:base + tf * 512 + 512],
                                    w_ap[:, hc * 128:(hc + 1) * 128],
                                    XT.ap()[:, hc * T + tf * 512:hc * T + tf * 512 + 512],
                                    start=(hc == 0), stop=False)
                        for ri in range(2):
                            rrow = FM.ap()[0:1, ri * T:(ri + 1) * T]
                            for tf in range(NTF):
                                inst = t_.matmul(
                                    PS[:, base + tf * 512:base + tf * 512 + 512],
                                    NEGR, rrow[0:1, tf * 512:tf * 512 + 512],
                                    start=False, stop=(ri == 1))
                    gmm_ev[j] = ePE.inc(inst)

                @block.scalar
                def _(s, ib=ib, j=j, half=half, spl=spill_ev):
                    s.wait_ge(ePE.h, gmm_ev[j])
                    if j == 0 and half == 0:
                        for (sem_, val_) in x1st_fence:
                            s.wait_ge(sem_.h, val_)
                    if j == 0 and half == 1 and spl:
                        s.wait_ge(qOUT.h, spl)
                    if j >= 1:
                        s.wait_ge(ePO.h, h_ev[j - 1])
                    PS = P0.ap() if j % 2 == 0 else P1.ap()
                    gu_act[j] = eACT.inc(s.activation(
                        FSf[:, 0:T], PS[:, 0:T], AF.Sigmoid,
                        scale=SGS.ap()[:, ib:ib + 1]))

                @block.vector
                def _(v, ib=ib, j=j, half=half, php=prev_half_pe):
                    if j >= 1:
                        v.wait_ge(ePO.h, gu_ev[j - 1])
                    if j == 0 and half == 1:
                        v.wait_ge(ePE.h, php)
                    v.wait_ge(ePE.h, gmm_ev[j])
                    vchain(v, v.tensor_scalar(FSf[:, 2 * T:3 * T],
                                              (P0.ap() if j % 2 == 0 else P1.ap())[:, T:2 * T],
                                              SUS.ap()[:, ib:ib + 1], None, ALU.mult))
                    v.wait_ge(eACT.h, gu_act[j])
                    gu_dve[j] = eDVE.inc(v.tensor_scalar(
                        FSf[:, T:2 * T],
                        (P0.ap() if j % 2 == 0 else P1.ap())[:, 0:T],
                        SGS.ap()[:, ib:ib + 1], None, ALU.mult))

                @block.gpsimd
                def _(g, j=j, half=half, php=prev_half_pe):
                    g.wait_ge(eDVE.h, gu_dve[j])
                    if j == 0 and half == 1:
                        g.wait_ge(ePE.h, php)
                    gu_ev[j] = ePO.inc(g.tensor_tensor(
                        FSf[:, 3 * T:4 * T], FSf[:, T:2 * T],
                        FSf[:, 2 * T:3 * T], ALU.mult))
                    g.wait_ge(ePO.h, ePO.n)
                    h_ev[j] = ePO.inc(g.tensor_tensor(hslot(j), FSf[:, 3 * T:4 * T],
                                                      FSf[:, 0:T], ALU.mult))
                    if j >= IBH - 4:
                        jd = j - (IBH - 4)
                        sem = dsems[jd % 4]
                        if jd == 0 and half == 0:
                            g.wait_ge(qGA.h, pp_totA)
                            g.wait_ge(qGB.h, pp_totB)
                        draw_ev[jd] = sem.dma(g.dma_start(
                            out=dwslot(jd),
                            in_=dbf_d.ap()[(half * IBH + jd) * 128:
                                           (half * IBH + jd + 1) * 128, :]))
                        g.wait_ge(sem.h, draw_ev[jd])
                        if jd == 0:
                            g.wait_ge(qMG.h, sdsb_ev)
                        scale_ev[jd] = ePO.inc(g.tensor_tensor(
                            dwslot(jd), dwslot(jd), SDSB, ALU.mult))

            @block.tensor
            def _(t_):
                t_.wait_ge(ePO.h, h_ev[IBH - 1])
                t_.wait_ge(eACT.h, gu_act[IBH - 1])
                t_.wait_ge(eDVE.h, gu_dve[IBH - 1])
                for tf in range(NTF):
                    for j in range(IBH):
                        inst = t_.matmul(P0.ap()[32:33, tf * 512:tf * 512 + 512],
                                         ONEB.ap(), hslot(j)[:, tf * 512:tf * 512 + 512],
                                         start=(j == 0), stop=(j == IBH - 1),
                                         tile_position=(0, 32))
                ePE.inc(inst)
            rsh_pe = ePE.n

            @block.vector
            def _(v, rsh_pe=rsh_pe):
                v.wait_ge(ePE.h, rsh_pe)
                vchain(v, v.tensor_scalar(FSf[32:33, 0:T], P0.ap()[32:33, 0:T],
                                          -128.0, None, ALU.mult))
                vchain(v, v.tensor_copy(FM.ap()[32:33, 0:T], FSf[32:33, 0:T]))
                vchain(v, v.tensor_sub(FM.ap()[32:33, T:2 * T], FSf[32:33, 0:T],
                                       FM.ap()[32:33, 0:T]))
            rsh_dve = eDVE.n

            yin_ev = {}
            if half == 0:
                @block.sync
                def _(sy, rshd=rsh_dve):
                    sy.wait_ge(eDVE.h, rshd)
                    for tb in range(TT):
                        sem = yd_sems[tb]
                        sy.wait_ge(sem.h, sem.n)
                        yin_ev[tb] = sem.dma(sy.dma_start(
                            out=yacc(tb),
                            in_=x1_d.ap()[tb * 128:(tb + 1) * 128, :]))
            else:
                if TT > 6:
                    @block.sync
                    def _(sy, rshd=rsh_dve, spl=spill_ev):
                        sy.wait_ge(eDVE.h, rshd)
                        sy.wait_ge(qOUT.h, spl)
                        for tb in (6, 7):
                            sem = yd_sems[tb]
                            sy.wait_ge(sem.h, sem.n)
                            yin_ev[tb] = sem.dma(sy.dma_start(
                                out=yacc(tb),
                                in_=y67_d.ap()[(tb - 6) * 128:(tb - 5) * 128, :]))

            groups = [list(range(i, min(i + GD, IBH))) for i in range(0, IBH, GD)]
            ngr = len(groups)
            dmm_ev = {}

            for j in range(4, min(IBH, 8)):
                @block.gpsimd
                def _(g, j=j, half=half):
                    sem = dsems[j % 4]
                    if j == 4:
                        g.wait_ge(ePE.h, gmm_ev[IBH - 1])
                    draw_ev[j] = sem.dma(g.dma_start(
                        out=dwslot(j),
                        in_=dbf_d.ap()[(half * IBH + j) * 128:
                                       (half * IBH + j + 1) * 128, :]))
                    g.wait_ge(sem.h, draw_ev[j])
                    scale_ev[j] = ePO.inc(g.tensor_tensor(
                        dwslot(j), dwslot(j), SDSB, ALU.mult))

            for gi, grp5 in enumerate(groups):
                for j in grp5:
                    if j < 8:
                        continue
                    @block.gpsimd
                    def _(g, j=j, half=half):
                        sem = dsems[j % 4]
                        g.wait_ge(ePE.h, dmm_ev[((j - 8) // GD, TT - 1)])
                        draw_ev[j] = sem.dma(g.dma_start(
                            out=dwslot(j),
                            in_=dbf_d.ap()[(half * IBH + j) * 128:
                                           (half * IBH + j + 1) * 128, :]))
                        g.wait_ge(sem.h, draw_ev[j])
                        scale_ev[j] = ePO.inc(g.tensor_tensor(
                            dwslot(j), dwslot(j), SDSB, ALU.mult))

                for tb in range(TT):
                    @block.tensor
                    def _(t_, gi=gi, grp5=grp5, tb=tb, rshd=rsh_dve, half=half):
                        t_.wait_ge(ePO.h, scale_ev[grp5[-1]])
                        idx = gi * TT + tb
                        ps = P0.ap() if idx % 2 == 0 else P1.ap()
                        if idx < 2:
                            t_.wait_ge(eDVE.h, rshd)
                        if idx >= 2:
                            kg, ktb = divmod(idx - 2, TT)
                            t_.wait_ge(eDVE.h, yadd_ev[(half, kg, ktb)])
                        elif half == 1:
                            t_.wait_ge(eDVE.h, yadd_ev[(0, ngr - 1, TT - 2 + idx)])
                        last = (gi == ngr - 1)
                        ng5 = len(grp5)
                        for i_loc, j in enumerate(grp5):
                            for fq in range(FQN):
                                inst = t_.matmul(
                                    ps[:, fq * 512:(fq + 1) * 512],
                                    hslot(j)[:, tb * 128:(tb + 1) * 128],
                                    dwslot(j)[:, fq * 512:fq * 512 + 512],
                                    start=(i_loc == 0),
                                    stop=(not last and i_loc == ng5 - 1 and fq == FQN - 1))
                        if last:
                            for ri in range(2):
                                rrow = FM.ap()[32:33, ri * T + tb * 128:ri * T + (tb + 1) * 128]
                                for fq in range(FQN):
                                    inst = t_.matmul(
                                        ps[:, fq * 512:(fq + 1) * 512], rrow,
                                        SDSB[32:33, fq * 512:(fq + 1) * 512],
                                        start=False,
                                        stop=(ri == 1 and fq == FQN - 1),
                                        tile_position=(32, 0))
                        dmm_ev[(gi, tb)] = ePE.inc(inst)

                    @block.vector
                    def _(v, gi=gi, tb=tb, half=half):
                        v.wait_ge(ePE.h, dmm_ev[(gi, tb)])
                        if gi == 0 and (half == 0 or (TT > 6 and tb in (6, 7))):
                            v.wait_ge(yd_sems[tb].h, yin_ev[tb])
                        idx = gi * TT + tb
                        ps = P0.ap() if idx % 2 == 0 else P1.ap()
                        yadd_ev[(half, gi, tb)] = eDVE.inc(v.tensor_tensor(
                            yacc(tb), ps[:, 0:H], yacc(tb), ALU.add))

            if half == 0 and TT > 6:
                @block.sync
                def _(sy, ngr=ngr):
                    for tb in (6, 7):
                        sy.wait_ge(eDVE.h, yadd_ev[(0, ngr - 1, tb)])
                    qOUT.dma(sy.dma_start(out=y67_d.ap()[0:128, :], in_=FSf[:, 0:H]))
                    qOUT.dma(sy.dma_start(out=y67_d.ap()[128:256, :], in_=FSf[:, H:2 * H]))
                spill_ev = qOUT.n
            prev_half_pe = ePE.n
            prev_half_dve = eDVE.n
            prev_half_d6 = dmm_ev[(max(ngr - 2, 0), TT - 1)]

        ngr2 = (IBH + GD - 1) // GD

        @block.sync
        def _(sy, ngr2=ngr2):
            for tb in range(TT):
                sy.wait_ge(eDVE.h, yadd_ev[(1, ngr2 - 1, tb)])
                qOUT.dma(sy.dma_start(
                    out=out_d.ap()[tb * 128:(tb + 1) * 128, :], in_=yacc(tb)))

        @block.gpsimd
        def _(g):
            for s_ in [qLD, qSA, qSB, qOUT, qGA, qGB, qMG, qD0, qD1, qP0, qP1, qPP,
                       qK_[0][0], qK_[0][1], qK_[1][0], qK_[1][1],
                       qV_[0], qV_[1], qE_[0], qE_[1], qE_[2], qE_[3]]:
                g.wait_ge(s_.h, s_.n)

    return nc


def _tile4(w):
    n0, n1 = w.shape
    a = w.reshape(n0 // 128, 128, n1 // 128, 128).transpose(0, 3, 2, 1)
    return np.ascontiguousarray(a).reshape(n0 // 128, 128, n1)


def prep_inputs(cfg, inputs):
    S = cfg["S"]; HEADS = cfg["HEADS"]; I = cfg["I"]
    H = 64 * HEADS
    T = S // NCORES
    IB = I // 128
    x = np.ascontiguousarray(np.asarray(inputs["L0_attn_hidden_states"],
                                        dtype=np.float32).reshape(S, H))
    wq = _tile4(np.asarray(inputs["L0_attn_wq"], dtype=np.float32))
    wk = _tile4(np.asarray(inputs["L0_attn_wk"], dtype=np.float32))
    wv = _tile4(np.asarray(inputs["L0_attn_wv"], dtype=np.float32))
    wo2 = np.ascontiguousarray(np.asarray(inputs["L0_attn_wo"], dtype=np.float32).T)
    relp = np.ascontiguousarray(np.asarray(inputs["L0_attn_rel_pos"], dtype=np.float32))
    gt = _tile4(np.asarray(inputs["L0_ffn_gate_w"], dtype=np.int32))
    ut = _tile4(np.asarray(inputs["L0_ffn_up_w"], dtype=np.int32))
    dt_ = np.ascontiguousarray(
        np.asarray(inputs["L0_ffn_down_w"], dtype=np.int32).T).reshape(IB, 128, H)
    gs = np.ascontiguousarray(
        np.asarray(inputs["L0_ffn_gate_scale"], dtype=np.float32).reshape(IB, 128).T)
    us = np.ascontiguousarray(
        np.asarray(inputs["L0_ffn_up_scale"], dtype=np.float32).reshape(IB, 128).T)
    ds = np.ascontiguousarray(
        np.asarray(inputs["L0_ffn_down_scale"], dtype=np.float32).reshape(1, H))
    ident = np.eye(128).astype(ml_dtypes.bfloat16)
    anti = np.ascontiguousarray(np.eye(128)[::-1]).astype(ml_dtypes.bfloat16)

    in_maps = []
    for c in range(NCORES):
        sel = np.zeros((8, 128), np.float32)
        sel[c, :] = 1.0
        in_maps.append({
            "x": np.ascontiguousarray(x[c * T:(c + 1) * T]),
            "wq_t": wq, "wk_t": wk, "wv_t": wv, "wo_t2": wo2,
            "rel_pos": relp,
            "gate_t": gt, "up_t": ut, "down_t": dt_,
            "gate_s": gs, "up_s": us, "down_s": ds,
            "ident": ident, "anti": anti,
            "sel": sel.astype(ml_dtypes.bfloat16),
        })
    return in_maps


def kernel(**inputs):
    cfg = FULL_CFG
    nc = build(cfg)
    in_maps = prep_inputs(cfg, inputs)
    res = run_bass_kernel_spmd(nc, in_maps, core_ids=list(range(NCORES)))
    S = cfg["S"]; H = 64 * cfg["HEADS"]
    out = np.concatenate([r["out"] for r in res.results], axis=0)
    return out.reshape(1, S, H).astype(np.float32)


# version 16
# speedup vs baseline: 1.0279x; 1.0279x over previous
import numpy as np
import ml_dtypes
from contextlib import ExitStack

import concourse.bass as bass
import concourse.mybir as mybir
import concourse.bass_utils as _bass_utils
from concourse.bass_utils import run_bass_kernel_spmd

if not getattr(_bass_utils, "_ldw_opt_patched", False):
    _orig_run_command = _bass_utils.run_command

    def _run_command_ldw(argv, **kwargs):
        argv = ["--enable-ldw-opt=true" if a == "--enable-ldw-opt=false" else a
                for a in argv]
        return _orig_run_command(argv, **kwargs)

    _bass_utils.run_command = _run_command_ldw
    _bass_utils._ldw_opt_patched = True

F32 = mybir.dt.float32
BF16 = mybir.dt.bfloat16
I32 = mybir.dt.int32
AF = mybir.ActivationFunctionType
ALU = mybir.AluOpType
AX = mybir.AxisListType

NCORES = 8
QF = 512
EPS = 1e-5

FULL_CFG = dict(S=8192, HEADS=32, I=8192)


class Sem:
    def __init__(self, ctx, nc, name):
        self.h = ctx.enter_context(nc.semaphore(name))
        self.n = 0

    def inc(self, inst, k=1):
        inst.then_inc(self.h, k)
        self.n += k
        return self.n

    def dma(self, inst):
        return self.inc(inst, 16)


def build(cfg):
    S = cfg["S"]; HEADS = cfg["HEADS"]; I = cfg["I"]
    H = 64 * HEADS
    T = S // NCORES
    FB = H // 128
    HC = H // 128
    KC = S // 128
    KCQ = KC // 4
    SQ = S // 4
    QH = T // QF
    NG = HEADS // 4
    TT = T // 128
    IB = I // 128
    ELEN = 2 * S - 1
    WLEN = S + T - 1
    ROWP = WLEN + 129
    GD = 4
    NW = (WLEN + 511) // 512
    FQN = H // 512
    NB4 = FB // 4
    NTH2 = T // 512
    assert T % QF == 0 and KC % 4 == 0 and H % 512 == 0 and FB % 4 == 0

    nc = bass.Bass("TRN2", target_bir_lowering=False)

    x_in = nc.dram_tensor("x", [T, H], F32, kind="ExternalInput")
    wq_t = nc.dram_tensor("wq_t", [FB, 128, H], F32, kind="ExternalInput")
    wk_t = nc.dram_tensor("wk_t", [FB, 128, H], F32, kind="ExternalInput")
    wv_t = nc.dram_tensor("wv_t", [FB, 128, H], F32, kind="ExternalInput")
    wo_t2 = nc.dram_tensor("wo_t2", [H, H], F32, kind="ExternalInput")
    rel = nc.dram_tensor("rel_pos", [1023, 64], F32, kind="ExternalInput")
    gate_t = nc.dram_tensor("gate_t", [IB, 128, H], I32, kind="ExternalInput")
    up_t = nc.dram_tensor("up_t", [IB, 128, H], I32, kind="ExternalInput")
    down_t = nc.dram_tensor("down_t", [IB, 128, H], I32, kind="ExternalInput")
    gate_s = nc.dram_tensor("gate_s", [128, IB], F32, kind="ExternalInput")
    up_s = nc.dram_tensor("up_s", [128, IB], F32, kind="ExternalInput")
    down_s = nc.dram_tensor("down_s", [1, H], F32, kind="ExternalInput")
    ident_in = nc.dram_tensor("ident", [128, 128], BF16, kind="ExternalInput")
    anti_in = nc.dram_tensor("anti", [128, 128], BF16, kind="ExternalInput")
    sel_in = nc.dram_tensor("sel", [8, 128], BF16, kind="ExternalInput")

    out_d = nc.dram_tensor("out", [T, H], F32, kind="ExternalOutput")

    kc_d = nc.dram_tensor("kc_d", [H, T], BF16)
    vc_d = nc.dram_tensor("vc_d", [T, 65 * HEADS], BF16)
    kg_d = nc.dram_tensor("kg_d", [8 * H, T], BF16, addr_space="Shared")
    vg_d = nc.dram_tensor("vg_d", [S, 65 * HEADS], BF16, addr_space="Shared")
    ef_d = nc.dram_tensor("ef_d", [ELEN + 1], F32)
    esh_d = nc.dram_tensor("esh_d", [128 * (ROWP + 1)], BF16)
    x1_d = nc.dram_tensor("x1_d", [T, H], F32)
    sc_d = nc.dram_tensor("sc_d", [4], F32)
    y67_d = nc.dram_tensor("y67_d", [256, H], F32)
    gbf_d = nc.dram_tensor("gbf_d", [IB * 128, H], BF16)
    ubf_d = nc.dram_tensor("ubf_d", [IB * 128, H], BF16)
    dbf_d = nc.dram_tensor("dbf_d", [IB * 128, H], BF16)
    wobf_d = nc.dram_tensor("wobf_d", [H, H], BF16)

    with ExitStack() as ctx:
        e = ctx.enter_context

        QT_C = max(FB * T, min(IB, 32) * 512, min(HC, 8) * H, WLEN + 1)
        CT_C = max(FB * T, max(0, IB - 32) * 512, WLEN + 1)
        XT_C = max(FB * T, 2 * KCQ * 260)
        KB_C = max(4 * SQ, 4 * H, 2 * H)
        WB_C = (2 * GD + 1) * H
        EX_C = max(4 * QF, H)
        AP_C = max(8 * QF, 65 * HEADS + T, 2 * H)

        QT = e(nc.sbuf_tensor("QT", [128, QT_C], BF16))
        CT = e(nc.sbuf_tensor("CT", [128, CT_C], BF16))
        XT = e(nc.sbuf_tensor("XT", [128, XT_C], BF16))
        KB = e(nc.sbuf_tensor("KB", [128, KB_C], BF16))
        WB = e(nc.sbuf_tensor("WB", [128, WB_C], BF16))
        EX = e(nc.sbuf_tensor("EX", [128, EX_C], BF16))
        APB = e(nc.sbuf_tensor("APB", [128, AP_C], BF16))
        ATB = e(nc.sbuf_tensor("ATB", [128, AP_C], BF16))
        XF = e(nc.sbuf_tensor("XF", [128, 2 * H], F32))
        FS = e(nc.sbuf_tensor("FS", [128, 4 * T], F32))
        FM = e(nc.sbuf_tensor("FM", [128, 2048], BF16))
        SID = e(nc.sbuf_tensor("SID", [128, 128], BF16))
        SAN = e(nc.sbuf_tensor("SAN", [128, 128], BF16))
        SSEL = e(nc.sbuf_tensor("SSEL", [8, 128], BF16))
        SREL = e(nc.sbuf_tensor("SREL", [128, 512], F32))
        SBV = e(nc.sbuf_tensor("SBV", [128, 8], F32))
        SBVB = e(nc.sbuf_tensor("SBVB", [128, 8], BF16))
        SGS = e(nc.sbuf_tensor("SGS", [128, IB], F32))
        SUS = e(nc.sbuf_tensor("SUS", [128, IB], F32))
        SCN = e(nc.sbuf_tensor("SCN", [128, 16], F32))
        ONEB = e(nc.sbuf_tensor("ONEB", [128, 1], BF16))
        ONEF = e(nc.sbuf_tensor("ONEF", [65, 64], F32))
        NEGR = ONEF.ap()[0:1, :].bitcast(BF16)

        P0 = e(nc.psum_tensor("P0", [128, 2048], F32))
        P1 = e(nc.psum_tensor("P1", [128, 2048], F32))

        qLD = Sem(ctx, nc, "qLD")
        qSA = Sem(ctx, nc, "qSA")
        qSB = Sem(ctx, nc, "qSB")
        qOUT = Sem(ctx, nc, "qOUT")
        qK_ = [[Sem(ctx, nc, f"qK{a}{b}") for b in range(2)] for a in range(2)]
        qV_ = [Sem(ctx, nc, f"qV{b}") for b in range(2)]
        qE_ = [Sem(ctx, nc, f"qE{b}") for b in range(4)]
        qGA = Sem(ctx, nc, "qGA")
        qGB = Sem(ctx, nc, "qGB")
        qMG = Sem(ctx, nc, "qMG")
        qD0 = Sem(ctx, nc, "qD0")
        qD1 = Sem(ctx, nc, "qD1")
        qPP = Sem(ctx, nc, "qPP")
        qWO = Sem(ctx, nc, "qWO")
        qP0 = Sem(ctx, nc, "qP0")
        qP1 = Sem(ctx, nc, "qP1")
        qP_ = [qP0, qP1]
        qS_ = [qSA, qSB]
        qG_ = [qGA, qGB]
        ePE = Sem(ctx, nc, "ePE")
        eACT = Sem(ctx, nc, "eACT")
        eDVE = Sem(ctx, nc, "eDVE")
        ePO = Sem(ctx, nc, "ePO")
        sCC = Sem(ctx, nc, "sCC")

        block = e(nc.Block())

        FSf = FS.ap()
        XFf = XF.ap()
        FMf = FM.ap()[:, 0:2048].bitcast(F32)
        P1b = P1.ap()[:, 0:512].bitcast(BF16)[:, 0:512]

        def xslot(i):
            return XFf[:, (i % 2) * H:((i % 2) * H + H)]

        def xlnslot(i):
            return APB.ap()[:, (i % 2) * H:((i % 2) * H + H)]

        def yacc(tb):
            if tb == 0:
                return APB.ap()[:, 0:2 * H].bitcast(F32)[:, 0:H]
            if tb == 1:
                return ATB.ap()[:, 0:2 * H].bitcast(F32)[:, 0:H]
            if tb in (2, 3):
                return XFf[:, (tb - 2) * H:((tb - 2) * H + H)]
            if tb in (4, 5):
                return KB.ap()[:, 0:4 * H].bitcast(F32)[:, (tb - 4) * H:(tb - 3) * H]
            return FSf[:, (tb - 6) * H:(tb - 5) * H]

        def hchunk(ib):
            if ib < 32:
                return QT.ap()[:, ib * 512:(ib + 1) * 512]
            return CT.ap()[:, (ib - 32) * 512:(ib - 31) * 512]

        def woslot(hc):
            if hc < min(HC, 8):
                return QT.ap()[:, hc * H:(hc + 1) * H]
            if hc < 12:
                return KB.ap()[:, (hc - 8) * H:(hc - 7) * H]
            return WB.ap()[:, (hc - 12) * H:(hc - 11) * H]

        def vchain(v, inst):
            eDVE.inc(inst)
            v.wait_ge(eDVE.h, eDVE.n)

        def newton_rsqrt(v):
            vv = SCN.ap()[:, 8:9]
            vh = SCN.ap()[:, 12:13]
            yy = SCN.ap()[:, 9:10]
            t1 = SCN.ap()[:, 13:14]
            vchain(v, v.tensor_scalar_mul(vh, vv, 0.5))
            b = vv.bitcast(I32)
            yb = yy.bitcast(I32)
            vchain(v, v.tensor_single_scalar(yb, b, 1, ALU.logical_shift_right))
            vchain(v, v.tensor_tensor(yb, SCN.ap()[:, 15:16].bitcast(I32), yb, ALU.subtract))
            for _ in range(3):
                vchain(v, v.tensor_tensor(t1, yy, yy, ALU.mult))
                vchain(v, v.tensor_tensor(t1, t1, vh, ALU.mult))
                vchain(v, v.tensor_scalar(t1, t1, -1.0, 1.5, ALU.mult, ALU.add))
                vchain(v, v.tensor_tensor(yy, yy, t1, ALU.mult))

        def emit_ln(v, src, dst_bf16, scratch=None):
            if scratch is None:
                scratch = P0.ap()[:, 0:H]
            mean = SCN.ap()[:, 0:1]
            ssq = SCN.ap()[:, 1:2]
            vchain(v, v.reduce_sum(mean, src, axis=AX.X))
            vchain(v, v.tensor_scalar_mul(mean, mean, 1.0 / H))
            vchain(v, v.scalar_tensor_tensor(scratch, src, mean, src,
                                             ALU.subtract, ALU.mult, accum_out=ssq))
            vchain(v, v.tensor_scalar(SCN.ap()[:, 8:9], ssq, 1.0 / H, EPS,
                                      ALU.mult, ALU.add))
            newton_rsqrt(v)
            vchain(v, v.tensor_scalar(dst_bf16, src, mean, SCN.ap()[:, 9:10],
                                      ALU.subtract, ALU.mult))

        @block.gpsimd
        def _(g):
            g.memset(SREL.ap(), 0.0)
            g.memset(ONEB.ap(), 1.0)
            g.memset(NEGR, -128.0)
            ePO.inc(g.memset(ONEF.ap()[64:65, :], 1.0))
        po_consts = ePO.n

        @block.sync
        def _(sy):
            qLD.dma(sy.dma_start(out=SID.ap(), in_=ident_in.ap()))
            qLD.dma(sy.dma_start(out=SAN.ap(), in_=anti_in.ap()))
            qLD.dma(sy.dma_start(out=SSEL.ap(), in_=sel_in.ap()))
            qLD.dma(sy.dma_start(out=SGS.ap(), in_=gate_s.ap()))
            qLD.dma(sy.dma_start(out=SUS.ap(), in_=up_s.ap()))
            sy.wait_ge(ePO.h, po_consts)
            qLD.dma(sy.dma_start(
                out=SREL.ap()[:, 0:448].rearrange("p (g d) -> p g d", g=7),
                in_=rel.ap()[0:896, :].rearrange("(g p) d -> p g d", p=128)))
            qLD.dma(sy.dma_start(out=SREL.ap()[0:127, 448:512],
                                 in_=rel.ap()[896:1023, :]))
        ld_setup = qLD.n

        @block.vector
        def _(v):
            v.wait_ge(qLD.h, ld_setup)
            for g8 in range(8):
                eDVE.inc(v.reduce_sum(SBV.ap()[:, g8:g8 + 1],
                                      SREL.ap()[:, g8 * 64:(g8 + 1) * 64], axis=AX.X))
            v.wait_ge(eDVE.h, eDVE.n)
            vchain(v, v.tensor_scalar_mul(SBV.ap(), SBV.ap(), 1.0 / 64.0))
            vchain(v, v.tensor_copy(SBVB.ap(), SBV.ap()))
            vchain(v, v.memset(SCN.ap()[:, 15:16].bitcast(I32), 0x5F3759DF))
        dv_bv = eDVE.n

        @block.tensor
        def _(t):
            t.wait_ge(eDVE.h, dv_bv)
            for g8 in range(8):
                inst = t.matmul(P1.ap()[0:1, (7 - g8) * 128:(8 - g8) * 128],
                                SBVB.ap()[:, g8:g8 + 1], SAN.ap(),
                                start=True, stop=True)
            ePE.inc(inst)
        pe_rev = ePE.n

        @block.scalar
        def _(s):
            s.wait_ge(ePE.h, pe_rev)
            eACT.inc(s.copy(FSf[0:1, 0:1024], P1.ap()[0:1, 0:1024]))
        act_rev = eACT.n

        RE = (ELEN + 1) // 1024
        r1, c1 = divmod(S - 512, 1024)

        @block.sync
        def _(sy):
            sy.wait_ge(eACT.h, act_rev)
            qLD.dma(sy.dma_start(out=sc_d.ap()[0:1], in_=FSf[0:1, 1:2],
                                 allow_slow_non_contiguous=True))
            qLD.dma(sy.dma_start(out=sc_d.ap()[1:2], in_=FSf[0:1, 1023:1024],
                                 allow_slow_non_contiguous=True))
            sy.wait_ge(qLD.h, qLD.n)
            qLD.dma(sy.dma_start(out=SCN.ap()[0:r1 + 1, 10:11],
                                 in_=bass.AP(sc_d, 0, [[0, r1 + 1], [1, 1]]),
                                 allow_slow_non_contiguous=True))
            qLD.dma(sy.dma_start(out=SCN.ap()[r1 + 1:RE, 10:11],
                                 in_=bass.AP(sc_d, 1, [[0, RE - (r1 + 1)], [1, 1]]),
                                 allow_slow_non_contiguous=True))
        ld_scol = qLD.n

        @block.vector
        def _(v):
            v.wait_ge(qLD.h, ld_scol)
            SEF2 = XFf[0:RE, 0:1024]
            vchain(v, v.memset(SEF2, 0.0))
            vchain(v, v.tensor_scalar_add(SEF2, SEF2, SCN.ap()[0:RE, 10:11]))
        dv_sef = eDVE.n

        @block.sync
        def _(sy):
            sy.wait_ge(eDVE.h, dv_sef)
            pos = 0
            rr, cc = r1, c1
            while pos < 1023:
                seg = min(1023 - pos, 1024 - cc)
                qLD.dma(sy.dma_start(out=XFf[rr:rr + 1, cc:cc + seg],
                                     in_=FSf[0:1, 1 + pos:1 + pos + seg]))
                pos += seg
                rr += 1
                cc = 0
            sy.wait_ge(qLD.h, qLD.n)
            qLD.dma(sy.dma_start(
                out=ef_d.ap().rearrange("(r c) -> r c", c=1024),
                in_=XFf[0:RE, 0:1024]))
        ld_ef = qLD.n

        @block.gpsimd
        def _(g):
            g.wait_ge(qLD.h, ld_ef)
            qMG.dma(g.dma_start(out=QT.ap()[0:8, 0:WLEN],
                                in_=bass.AP(ef_d, 0, [[T, 8], [1, WLEN]])))
        lg_cand = qMG.n

        ppin_ev = {}
        pp_early = []
        for i_ in range(min(8, IB // 2)):
            pp_early += [i_, IB + i_]
        pp_early_set = set(pp_early)
        pp_order = pp_early + [k for k in range(3 * IB) if k not in pp_early_set]
        pp_seg = pp_order

        pe_w0 = ePE.n
        act_w0 = eACT.n

        @block.tensor
        def _(t):
            t.wait_ge(qMG.h, lg_cand)
            t.wait_ge(eACT.h, act_rev)
            for w in range(NW):
                lo = w * 512
                ln = min(512, WLEN - lo)
                if w >= 2:
                    t.wait_ge(eACT.h, act_w0 + (w - 2) + 1)
                ePE.inc(t.matmul(P1.ap()[:, (w % 2) * 512:(w % 2) * 512 + ln],
                                 SSEL.ap(), QT.ap()[0:8, lo:lo + ln],
                                 start=True, stop=True))

        @block.scalar
        def _(s):
            for w in range(NW):
                lo = w * 512
                ln = min(512, WLEN - lo)
                s.wait_ge(ePE.h, pe_w0 + w + 1)
                eACT.inc(s.activation(CT.ap()[:, lo:lo + ln],
                                      P1.ap()[:, (w % 2) * 512:(w % 2) * 512 + ln],
                                      AF.Exp))
        act_wexp = eACT.n

        @block.sync
        def _(sy):
            sy.wait_ge(eACT.h, act_wexp)
            qLD.dma(sy.dma_start(out=bass.AP(esh_d, 0, [[ROWP + 1, 128], [1, WLEN]]),
                                 in_=CT.ap()[:, 0:WLEN]))
        ld_esh = qLD.n

        @block.gpsimd
        def _(g):
            g.wait_ge(qLD.h, ld_esh)
            for hc in range(HC):
                qP0.dma(g.dma_start(
                    out=wobf_d.ap()[hc * 128:(hc + 1) * 128, :],
                    in_=wo_t2.ap()[hc * 128:(hc + 1) * 128, :]))
        wocast_tot = qP0.n

        def emit_transposes(src_tile_fn, dst_col_fn, first_pe_waits):
            copies_base = eDVE.n
            pe_list = []

            @block.tensor
            def _(t_):
                for (sem, val) in first_pe_waits:
                    t_.wait_ge(sem, val)
                for b in range(NB4):
                    if b >= 1:
                        t_.wait_ge(eDVE.h, copies_base + b)
                    for j in range(4):
                        inst = t_.transpose(P1b[:, j * 128:(j + 1) * 128],
                                            src_tile_fn(b * 4 + j), SID.ap())
                    pe_list.append(ePE.inc(inst))

            @block.vector
            def _(v):
                for b in range(NB4):
                    v.wait_ge(ePE.h, pe_list[b])
                    eDVE.inc(v.tensor_copy(
                        dst_col_fn(b),
                        P1b.rearrange("p (f u) -> p f u", f=4)))
            assert eDVE.n == copies_base + NB4
            return pe_list[-1], eDVE.n

        ln_ev = {}
        tr_pe_ev = {}
        x_ev = {}
        ptcopy_last = 0
        for tt in range(TT):
            @block.sync
            def _(sy, tt=tt):
                if tt >= 2:
                    sy.wait_ge(eDVE.h, ln_ev[tt - 2])
                x_ev[tt] = qS_[tt % 2].dma(sy.dma_start(
                    out=xslot(tt), in_=x_in.ap()[tt * 128:(tt + 1) * 128, :]))

            @block.vector
            def _(v, tt=tt):
                v.wait_ge(qS_[tt % 2].h, x_ev[tt])
                if tt >= 2:
                    v.wait_ge(ePE.h, tr_pe_ev[tt - 2])
                emit_ln(v, xslot(tt), xlnslot(tt))
                ln_ev[tt] = eDVE.inc(v.engine_nop())

            waits = [(eDVE.h, ln_ev[tt])]
            if tt == 0:
                waits.append((qLD.h, ld_esh))
            else:
                waits.append((eDVE.h, ptcopy_last))

            def mksrc(tt):
                return lambda fb: xlnslot(tt)[:, fb * 128:(fb + 1) * 128]

            def mkdst(tt):
                return lambda b: (XT.ap()[:, 0:FB * T]
                                  .rearrange("p (f u) -> p f u", f=FB)
                                  [:, b * 4:b * 4 + 4, tt * 128:(tt + 1) * 128])

            tr_pe_ev[tt], ptcopy_last = emit_transposes(mksrc(tt), mkdst(tt), waits)
        ph1_end_dve = eDVE.n

        def emit_proj(wt, evac, first_waits=()):
            mm_ev = {}
            wl_ev = {}
            cast_ev = {}
            ev_done = {}

            def emit_load_cast(fb):
                @block.sync
                def _(sy, fb=fb):
                    if fb >= 2:
                        sy.wait_ge(eACT.h, cast_ev[fb - 2])
                    wl_ev[fb] = qV_[fb % 2].dma(sy.dma_start(
                        out=FSf[:, (fb % 2) * (2 * T):(fb % 2) * (2 * T) + H],
                        in_=wt.ap()[fb]))

                @block.scalar
                def _(s, fb=fb):
                    s.wait_ge(qV_[fb % 2].h, wl_ev[fb])
                    if fb >= 2:
                        s.wait_ge(ePE.h, mm_ev[fb - 2])
                    cast_ev[fb] = eACT.inc(s.copy(
                        WB.ap()[:, (fb % 2) * H:((fb % 2) * H + H)],
                        FSf[:, (fb % 2) * (2 * T):(fb % 2) * (2 * T) + H]))

            def emit_mm(fb):
                @block.tensor
                def _(t_, fb=fb):
                    t_.wait_ge(eACT.h, cast_ev[fb])
                    if fb == 0:
                        for (semh, val) in first_waits:
                            t_.wait_ge(semh, val)
                    if fb >= 1:
                        t_.wait_ge(eACT.h, ev_done[fb - 1])
                    for hc in range(HC):
                        for th2 in range(NTH2):
                            inst = t_.matmul(
                                P0.ap()[:, th2 * 512:(th2 + 1) * 512],
                                WB.ap()[:, (fb % 2) * H + hc * 128:(fb % 2) * H + (hc + 1) * 128],
                                XT.ap()[:, hc * T + th2 * 512:hc * T + th2 * 512 + 512],
                                start=(hc == 0), stop=(hc == HC - 1))
                    mm_ev[fb] = ePE.inc(inst)

            emit_load_cast(0)
            emit_mm(0)
            for fb in range(1, FB):
                emit_load_cast(fb)
                evac(fb - 1, mm_ev[fb - 1])
                ev_done[fb - 1] = eACT.n
                emit_mm(fb)
            evac(FB - 1, mm_ev[FB - 1])
            ev_done[FB - 1] = eACT.n
            return mm_ev

        kst_ev = {}

        def kstg(fb):
            base = 65 * HEADS
            return (APB.ap()[:, base:base + T] if fb % 2 == 0
                    else ATB.ap()[:, base:base + T])

        def kevac(fb, mmv):
            @block.scalar
            def _(s, fb=fb, mmv=mmv):
                s.wait_ge(ePE.h, mmv)
                if fb >= 2:
                    s.wait_ge(qS_[fb % 2].h, kst_ev[fb - 2])
                eACT.inc(s.copy(kstg(fb), P0.ap()[:, 0:NTH2 * 512]))
            a_now = eACT.n

            @block.sync
            def _(sy, fb=fb, a_now=a_now):
                sy.wait_ge(eACT.h, a_now)
                kst_ev[fb] = qS_[fb % 2].dma(sy.dma_start(
                    out=kc_d.ap()[fb * 128:(fb + 1) * 128, :], in_=kstg(fb)))

        def vevac(fb, mmv):
            @block.scalar
            def _(s, fb=fb, mmv=mmv):
                s.wait_ge(ePE.h, mmv)
                if fb == 0:
                    s.wait_ge(qLD.h, ld_esh)
                eACT.inc(s.copy(CT.ap()[:, fb * T:fb * T + T], P0.ap()[:, 0:NTH2 * 512]))

        def qevac(fb, mmv):
            @block.scalar
            def _(s, fb=fb, mmv=mmv):
                s.wait_ge(ePE.h, mmv)
                eACT.inc(s.activation(QT.ap()[:, fb * T:fb * T + T],
                                      P0.ap()[:, 0:NTH2 * 512], AF.Copy, scale=0.125))

        emit_proj(wk_t, kevac, first_waits=[(eDVE.h, ph1_end_dve)])

        kst_totA, kst_totB = qSA.n, qSB.n

        @block.gpsimd
        def _(g):
            g.wait_ge(qSA.h, kst_totA)
            g.wait_ge(qSB.h, kst_totB)
            sCC.inc(g.collective_compute(
                "AllGather", ALU.bypass, replica_groups=[list(range(NCORES))],
                ins=[kc_d.ap()], outs=[kg_d.ap()]))

        emit_proj(wv_t, vevac, first_waits=[(eACT.h, eACT.n)])
        v_all = eACT.n

        vst_ev = {}
        vtp_last = ptcopy_last
        for tb in range(TT):
            vcb = (APB.ap()[:, 0:65 * HEADS] if tb % 2 == 0
                   else ATB.ap()[:, 0:65 * HEADS])

            @block.gpsimd
            def _(g, tb=tb, vcb=vcb):
                if tb >= 2:
                    g.wait_ge(qS_[tb % 2].h, vst_ev[tb - 2])
                ePO.inc(g.memset(vcb, 1.0))
            po_now = ePO.n

            waits = [(eACT.h, v_all), (ePO.h, po_now), (eDVE.h, vtp_last)]

            def mksrcv(tb):
                return lambda fb: CT.ap()[:, fb * T + tb * 128:fb * T + tb * 128 + 128]

            def mkdstv(vcb):
                return lambda b: (vcb.rearrange("p (f u) -> p f u", f=HEADS // 2)
                                  [:, b * 4:b * 4 + 4, :]
                                  .rearrange("p f (g u) -> p f g u", g=2)[:, :, :, 0:64])

            copies_base = eDVE.n
            pe_list = []

            @block.tensor
            def _(t_, tb=tb):
                for (semh, val) in waits:
                    t_.wait_ge(semh, val)
                for b in range(NB4):
                    if b >= 1:
                        t_.wait_ge(eDVE.h, copies_base + b)
                    for j in range(4):
                        inst = t_.transpose(P1b[:, j * 128:(j + 1) * 128],
                                            mksrcv(tb)(b * 4 + j), SID.ap())
                    pe_list.append(ePE.inc(inst))

            @block.vector
            def _(v, tb=tb, vcb=vcb):
                for b in range(NB4):
                    v.wait_ge(ePE.h, pe_list[b])
                    eDVE.inc(v.tensor_copy(
                        mkdstv(vcb)(b),
                        P1b[:, 0:512].rearrange("p (f g u) -> p f g u", f=4, g=2)))
            vtp_last = eDVE.n

            @block.sync
            def _(sy, tb=tb, vcb=vcb, d_now=vtp_last):
                sy.wait_ge(eDVE.h, d_now)
                vst_ev[tb] = qS_[tb % 2].dma(sy.dma_start(
                    out=vc_d.ap()[tb * 128:(tb + 1) * 128, :], in_=vcb))

        vst_totA, vst_totB = qSA.n, qSB.n

        @block.gpsimd
        def _(g):
            g.wait_ge(qSA.h, vst_totA)
            g.wait_ge(qSB.h, vst_totB)
            sCC.inc(g.collective_compute(
                "AllGather", ALU.bypass, replica_groups=[list(range(NCORES))],
                ins=[vc_d.ap()], outs=[vg_d.ap()]))
        cc_all = sCC.n

        emit_proj(wq_t, qevac, first_waits=[(eACT.h, v_all)])
        q_mm_last = ePE.n
        q_all = eACT.n

        s01_evs, s23_evs, av_evs = {}, {}, {}
        eA_evs, eB_evs, m01_evs, m23_evs, eb_evs = {}, {}, {}, {}, {}
        woS_ev = {}

        @block.sync
        def _(sy):
            sy.wait_ge(ePE.h, q_mm_last)
            sy.wait_ge(qP0.h, wocast_tot)
            for hc in range(12, min(HC, 16)):
                woS_ev[hc] = qWO.dma(sy.dma_start(
                    out=woslot(hc), in_=wobf_d.ap()[hc * 128:(hc + 1) * 128, :]))
        NSEG = QH * NG
        SPB = -(-len(pp_seg) // NSEG) if pp_seg else 1
        ctxfree = vtp_last
        p0free = (eACT.h, q_all)
        tg = 0

        for qh in range(QH):
            for grp in range(NG):
                seg0 = tg
                pe_base, act_base, dve_base = ePE.n, eACT.n, eDVE.n
                for kc in range(KC):
                    t = seg0 + kc
                    if kc == 0:
                        s01_evs[t] = pe_base + 1
                        s23_evs[t] = pe_base + 2
                    else:
                        s01_evs[t] = pe_base + 3 * kc
                        s23_evs[t] = pe_base + 3 * kc + 1
                    av_evs[t] = pe_base + (3 * kc + 5 if kc < KC - 1 else 3 * KC)
                    eA_evs[t] = act_base + 2 * kc + 1
                    eB_evs[t] = act_base + 2 * kc + 2
                    m01_evs[t] = dve_base + 2 * kc + 1
                    m23_evs[t] = dve_base + 2 * kc + 2

                kq_ev = {}
                vq_ev = {}

                @block.sync
                def _(sy, grp=grp, seg0=seg0):
                    if seg0 == 0:
                        sy.wait_ge(sCC.h, cc_all)
                        sy.wait_ge(ePE.h, q_mm_last)
                    for q in range(4):
                        if q >= 2:
                            sy.wait_ge(ePE.h, s23_evs[seg0 + (q - 1) * KCQ - 1])
                        elif seg0 > 0:
                            sy.wait_ge(ePE.h, s23_evs[seg0 - 1 - (1 - q) * KCQ])
                        for pi in range(2):
                            pp = 2 * grp + pi
                            kq_ev[(pi, q)] = qK_[pi][q % 2].dma(sy.dma_start(
                                out=KB.ap()[:, (pi * 2 + q % 2) * SQ:(pi * 2 + q % 2) * SQ + SQ]
                                    .rearrange("p (r u) -> p r u", r=2),
                                in_=kg_d.ap().rearrange("(r h) u -> r h u", h=H)
                                    [2 * q:2 * q + 2, pp * 128:(pp + 1) * 128, :]
                                    .transpose([1, 0, 2])))
                        if q >= 2:
                            sy.wait_ge(ePE.h, av_evs[seg0 + (q - 1) * KCQ - 1])
                        elif seg0 > 0:
                            sy.wait_ge(ePE.h, av_evs[seg0 - 1 - (1 - q) * KCQ])
                        vq_ev[q] = qV_[q % 2].dma(sy.dma_start(
                            out=XT.ap()[:, (q % 2) * KCQ * 260:((q % 2) + 1) * KCQ * 260]
                                .rearrange("p (k u) -> p k u", k=KCQ),
                            in_=vg_d.ap().rearrange("(k p) u -> p k u", p=128)
                                [:, q * KCQ:(q + 1) * KCQ, grp * 260:(grp + 1) * 260]))

                si = seg0 // KC
                myk = pp_seg[si * SPB:(si + 1) * SPB]
                kq_sched = {}
                for xi, k in enumerate(myk):
                    kq_sched[min(KC - 1, 2 + xi * max(1, KC // max(1, len(myk))))] = k
                if si == NSEG - 1 and HC > 7 and seg0 > 0:
                    @block.sync
                    def _(sy, wo_prev=s23_evs[seg0 - 1]):
                        sy.wait_ge(ePE.h, wo_prev)
                        for hc in range(0, 7):
                            woS_ev[hc] = qWO.dma(sy.dma_start(
                                out=woslot(hc),
                                in_=wobf_d.ap()[hc * 128:(hc + 1) * 128, :]))

                @block.gpsimd
                def _(g, qh=qh, seg0=seg0, myk=myk, kq_sched=kq_sched):
                    for kc in range(KC):
                        t = seg0 + kc
                        if t == 0:
                            g.wait_ge(qLD.h, ld_esh)
                        if t >= 4:
                            g.wait_ge(eDVE.h, m23_evs[t - 4])
                        off = (S - 1) + 512 * qh - 128 * kc
                        eb_evs[t] = qE_[kc % 4].dma(g.dma_start(
                            out=EX.ap()[:, (kc % 4) * 512:(kc % 4) * 512 + 512],
                            in_=bass.AP(esh_d, off, [[ROWP, 128], [1, 512]])))
                        if kc in kq_sched:
                            k = kq_sched[kc]
                            srcs = (gate_t, up_t, down_t)
                            dsts = (gbf_d, ubf_d, dbf_d)
                            r = k % IB
                            csem = qPP if k in pp_early_set else qG_[k % 2]
                            ppin_ev[k] = csem.dma(g.dma_start(
                                out=dsts[k // IB].ap()[r * 128:(r + 1) * 128, :],
                                in_=srcs[k // IB].ap()[k % IB]))

                @block.tensor
                def _(t_, qh=qh, grp=grp, seg0=seg0, p0free=p0free, ctxfree=ctxfree):
                    for kc in range(KC):
                        t = seg0 + kc
                        q = kc // KCQ
                        kq = kc % KCQ
                        if kq == 0:
                            for pi in range(2):
                                t_.wait_ge(qK_[pi][q % 2].h, kq_ev[(pi, q)])
                        for half2 in range(2):
                            if kc == 0:
                                if half2 == 0:
                                    t_.wait_ge(*p0free)
                            else:
                                t_.wait_ge(eACT.h, (eA_evs if half2 == 0 else eB_evs)[t - 1])
                            for j in (0, 1) if half2 == 0 else (2, 3):
                                pi, sg = j // 2, j % 2
                                ks = (pi * 2 + q % 2) * SQ
                                inst = t_.matmul(
                                    P0.ap()[:, j * 512:(j + 1) * 512],
                                    KB.ap()[sg * 64:(sg + 1) * 64,
                                            ks + kq * 128:ks + (kq + 1) * 128],
                                    QT.ap()[sg * 64:(sg + 1) * 64,
                                            (2 * grp + pi) * T + qh * 512:(2 * grp + pi) * T + qh * 512 + 512],
                                    start=True, stop=True, tile_position=(sg * 64, 0))
                            assert ePE.inc(inst) == (s01_evs if half2 == 0 else s23_evs)[t]
                        kcps = [kc - 1, kc] if kc == KC - 1 else ([kc - 1] if kc >= 1 else [])
                        for kcp in kcps:
                            tp = seg0 + kcp
                            qp = kcp // KCQ
                            kqp = kcp % KCQ
                            if kqp == 0:
                                t_.wait_ge(qV_[qp % 2].h, vq_ev[qp])
                            if kcp == 0:
                                t_.wait_ge(eDVE.h, ctxfree)
                            for j in range(4):
                                if j == 0:
                                    t_.wait_ge(eDVE.h, m01_evs[tp])
                                if j == 2:
                                    t_.wait_ge(eDVE.h, m23_evs[tp])
                                vb = (qp % 2) * KCQ * 260 + kqp * 260 + j * 65
                                inst = t_.matmul(
                                    P1.ap()[0:65, j * 512:(j + 1) * 512],
                                    XT.ap()[:, vb:vb + 65],
                                    ATB.ap()[:, (tp % 2) * 2048 + j * 512:(tp % 2) * 2048 + (j + 1) * 512],
                                    start=(kcp == 0), stop=(kcp == KC - 1))
                            assert ePE.inc(inst) == av_evs[tp]

                @block.scalar
                def _(s, seg0=seg0):
                    for kc in range(KC):
                        t = seg0 + kc
                        s.wait_ge(ePE.h, s01_evs[t])
                        if t >= 2:
                            s.wait_ge(eDVE.h, m01_evs[t - 2])
                        assert eACT.inc(s.activation(
                            APB.ap()[:, (t % 2) * 2048:(t % 2) * 2048 + 1024],
                            P0.ap()[:, 0:1024], AF.Exp)) == eA_evs[t]
                        s.wait_ge(ePE.h, s23_evs[t])
                        if t >= 2:
                            s.wait_ge(eDVE.h, m23_evs[t - 2])
                        assert eACT.inc(s.activation(
                            APB.ap()[:, (t % 2) * 2048 + 1024:(t % 2) * 2048 + 2048],
                            P0.ap()[:, 1024:2048], AF.Exp)) == eB_evs[t]

                @block.vector
                def _(v, seg0=seg0):
                    for kc in range(KC):
                        t = seg0 + kc
                        v.wait_ge(qE_[kc % 4].h, eb_evs[t])
                        ebrep = bass.AP(EX.ap().tensor, (kc % 4) * 512,
                                        [[EX_C, 128], [0, 2], [1, 512]])
                        for half2 in range(2):
                            v.wait_ge(eACT.h, (eA_evs if half2 == 0 else eB_evs)[t])
                            if t >= 2:
                                v.wait_ge(ePE.h, av_evs[t - 2])
                            base = (t % 2) * 2048 + half2 * 1024
                            inst = v.tensor_tensor(
                                ATB.ap()[:, base:base + 1024].rearrange("p (r u) -> p r u", r=2),
                                APB.ap()[:, base:base + 1024].rearrange("p (r u) -> p r u", r=2),
                                ebrep, ALU.mult)
                            assert eDVE.inc(inst) == (m01_evs if half2 == 0 else m23_evs)[t]

                tlast = seg0 + KC - 1

                rec_base = eDVE.n

                @block.vector
                def _(v, tlast=tlast):
                    v.wait_ge(ePE.h, av_evs[tlast])
                    for j in range(4):
                        eDVE.inc(v.reciprocal(FSf[64:65, j * 512:(j + 1) * 512],
                                              P1.ap()[64:65, j * 512:(j + 1) * 512]))

                @block.tensor
                def _(t_, rb=rec_base, tlast=tlast):
                    t_.wait_ge(eACT.h, eB_evs[tlast])
                    for j in range(4):
                        t_.wait_ge(eDVE.h, rb + j + 1)
                        inst = t_.matmul(P0.ap()[0:64, j * 512:(j + 1) * 512],
                                         ONEF.ap()[64:65, :],
                                         FSf[64:65, j * 512:(j + 1) * 512],
                                         start=True, stop=True, tile_position=(64, 0))
                        ePE.inc(inst)
                bc_ev = ePE.n
                assert bc_ev == pe_base + 3 * KC + 4

                @block.vector
                def _(v, bc_ev=bc_ev, qh=qh, grp=grp):
                    v.wait_ge(ePE.h, bc_ev)
                    vchain(v, v.tensor_copy(FSf[0:64, 0:2048], P0.ap()[0:64, :]))
                    for j in range(4):
                        pi, sg = j // 2, j % 2
                        dst = CT.ap()[sg * 64:(sg + 1) * 64,
                                      (2 * grp + pi) * T + qh * 512:(2 * grp + pi) * T + qh * 512 + 512]
                        inst = v.tensor_tensor(dst, P1.ap()[0:64, j * 512:(j + 1) * 512],
                                               FSf[0:64, j * 512:(j + 1) * 512], ALU.mult)
                    eDVE.inc(inst)
                ctxfree = eDVE.n
                assert ctxfree == dve_base + 2 * KC + 6
                p0free = (eDVE.h, dve_base + 2 * KC + 5)
                tg += KC

        assert len(ppin_ev) == 3 * IB, (len(ppin_ev), 3 * IB)
        pp_totA, pp_totB = qGA.n, qGB.n
        qpp_tot = qPP.n
        last_s = s23_evs[tg - 1]

        wo_ev = {}

        @block.sync
        def _(sy):
            sy.wait_ge(ePE.h, last_s)
            sy.wait_ge(qP0.h, wocast_tot)
            order = [h for h in (8, 9, 10, 11, 7) if h < HC] + \
                    [h for h in range(HC) if h not in (7, 8, 9, 10, 11)]
            for hc in order:
                if hc in woS_ev:
                    continue
                woS_ev[hc] = qWO.dma(sy.dma_start(
                    out=woslot(hc), in_=wobf_d.ap()[hc * 128:(hc + 1) * 128, :]))

        op_ev = {}
        x1_ev = {}
        x1add_ev = {}
        x1st_ev = {}
        xr_ev = {}
        x1t_pe_ev = {}
        ptcopy_last4 = None

        def mksrc4(tb):
            return lambda fb: xlnslot(tb)[:, fb * 128:(fb + 1) * 128]

        def mkdst4(tb):
            return lambda b: (XT.ap()[:, 0:FB * T]
                              .rearrange("p (f u) -> p f u", f=FB)
                              [:, b * 4:b * 4 + 4, tb * 128:(tb + 1) * 128])

        def emit_x1t(tp):
            waits = [(eDVE.h, x1_ev[tp])]
            if tp == 0:
                waits.append((eDVE.h, ctxfree))
            else:
                waits.append((eDVE.h, ptcopy_last4))
            return emit_transposes(mksrc4(tp), mkdst4(tp), waits)

        for tb in range(TT):
            @block.sync
            def _(sy, tb=tb):
                if tb == 0:
                    sy.wait_ge(qSA.h, qSA.n)
                    sy.wait_ge(qSB.h, qSB.n)
                if tb >= 2:
                    sy.wait_ge(eDVE.h, x1_ev[tb - 2])
                xr_ev[tb] = qS_[tb % 2].dma(sy.dma_start(
                    out=xslot(tb), in_=x_in.ap()[tb * 128:(tb + 1) * 128, :]))

            @block.tensor
            def _(t_, tb=tb):
                if tb == 0:
                    t_.wait_ge(*p0free)
                    t_.wait_ge(eDVE.h, ctxfree)
                else:
                    t_.wait_ge(eDVE.h, x1add_ev[tb - 1])
                if HC > 12:
                    hc_order = [12, 13, 14, 15] + list(range(0, 7)) + [8, 9, 10, 11, 7]
                else:
                    hc_order = list(range(HC))
                for oi, hc in enumerate(hc_order):
                    if tb == 0 and oi == 0:
                        w0 = max(woS_ev[h2] for h2 in
                                 list(range(min(7, HC))) + [h for h in (12, 13, 14, 15) if h < HC])
                        t_.wait_ge(qWO.h, w0)
                    if tb == 0 and HC > 12 and hc in (7, 8, 9, 10, 11):
                        t_.wait_ge(qWO.h, woS_ev[hc])
                    for fq in range(FQN):
                        inst = t_.matmul(
                            P0.ap()[:, fq * 512:(fq + 1) * 512],
                            CT.ap()[:, hc * T + tb * 128:hc * T + tb * 128 + 128],
                            woslot(hc)[:, fq * 512:(fq + 1) * 512],
                            start=(oi == 0), stop=(oi == HC - 1))
                op_ev[tb] = ePE.inc(inst)

            @block.vector
            def _(v, tb=tb):
                v.wait_ge(ePE.h, op_ev[tb])
                v.wait_ge(qS_[tb % 2].h, xr_ev[tb])
                if tb >= 1:
                    v.wait_ge(qP1.h, x1st_ev[tb - 1])
                vchain(v, v.tensor_tensor(FSf[:, 0:H], P0.ap()[:, 0:H],
                                          xslot(tb), ALU.add))
                x1add_ev[tb] = eDVE.n

            if tb >= 1:
                x1t_pe_ev[tb - 1], ptcopy_last4 = emit_x1t(tb - 1)

            @block.vector
            def _(v, tb=tb):
                if tb >= 2:
                    v.wait_ge(ePE.h, x1t_pe_ev[tb - 2])
                emit_ln(v, FSf[:, 0:H], xlnslot(tb), scratch=FSf[:, H:2 * H])
                x1_ev[tb] = eDVE.inc(v.engine_nop())

            @block.scalar
            def _(s, tb=tb):
                s.wait_ge(eDVE.h, x1add_ev[tb])
                x1st_ev[tb] = qP1.dma(s.dma_start(
                    out=x1_d.ap()[tb * 128:(tb + 1) * 128, :], in_=FSf[:, 0:H]))

        x1t_pe_ev[TT - 1], ptcopy_last4 = emit_x1t(TT - 1)
        ph4_end_dve = eDVE.n
        ph4_end_pe = ePE.n
        x1st_fence = [(qP1, x1st_ev[TT - 1])]

        IBH = IB // 2
        NTF = T // 512
        assert IB % 2 == 0
        QTH = QT_C // T

        def hslot(j):
            if j < QTH:
                return QT.ap()[:, j * T:(j + 1) * T]
            return CT.ap()[:, (j - QTH) * T:(j - QTH + 1) * T]

        def guw(j, up):
            s_ = ((6 if up else 4) + j) if j < 2 else ((2 if up else 0) + j % 2)
            return WB.ap()[:, s_ * H:(s_ + 1) * H]

        def dwslot(j):
            s_ = (j + 4) % 8
            return WB.ap()[:, s_ * H:(s_ + 1) * H]

        SDSB = WB.ap()[:, 2 * GD * H:2 * GD * H + H]
        yd_sems = [qK_[0][0], qK_[0][1], qK_[1][0], qK_[1][1],
                   qV_[0], qV_[1], qE_[0], qE_[1]]

        @block.gpsimd
        def _(g):
            g.wait_ge(ePE.h, ph4_end_pe)
            qMG.dma(g.dma_start(out=SDSB,
                                in_=bass.AP(down_s, 0, [[0, 128], [1, H]])))
        sdsb_ev = qMG.n

        @block.tensor
        def _(t_):
            t_.wait_ge(eDVE.h, ph4_end_dve)
            for tf in range(NTF):
                for hc in range(HC):
                    inst = t_.matmul(P0.ap()[0:1, tf * 512:tf * 512 + 512], ONEB.ap(),
                                     XT.ap()[:, hc * T + tf * 512:hc * T + tf * 512 + 512],
                                     start=(hc == 0), stop=(hc == HC - 1))
            ePE.inc(inst)
        rsx_pe = ePE.n

        @block.vector
        def _(v):
            v.wait_ge(ePE.h, rsx_pe)
            for (sem_, val_) in x1st_fence:
                v.wait_ge(sem_.h, val_)
            vchain(v, v.tensor_copy(FSf[0:1, 0:T], P0.ap()[0:1, 0:T]))
            vchain(v, v.tensor_copy(FM.ap()[0:1, 0:T], FSf[0:1, 0:T]))
            vchain(v, v.tensor_sub(FM.ap()[0:1, T:2 * T], FSf[0:1, 0:T],
                                   FM.ap()[0:1, 0:T]))
        rsx_dve = eDVE.n

        prev_half_pe = ph4_end_pe
        prev_half_dve = eDVE.n
        prev_half_d6 = 0
        spill_ev = 0
        yadd_ev = {}
        dsems = [qGB, qD0, qD1, qMG]
        for half in range(2):
            draw_ev = {}
            scale_ev = {}
            gul_ev = {}
            gmm_ev = {}
            gu_act = {}
            gu_dve = {}
            gu_ev = {}
            h_ev = {}
            for j in range(IBH):
                ib = half * IBH + j

                @block.sync
                def _(sy, ib=ib, j=j, ph6=prev_half_d6, half=half):
                    if j >= 2:
                        sy.wait_ge(ePE.h, gmm_ev[j - 2])
                    else:
                        sy.wait_ge(qV_[j % 2].h, qV_[j % 2].n)
                        if half == 1:
                            sy.wait_ge(ePE.h, ph6)
                    if half == 0 and j < 8:
                        sy.wait_ge(qPP.h, qpp_tot)
                    elif (half == 0 and j in (8, 9)) or (half == 1 and j < 2):
                        sy.wait_ge(qGA.h, pp_totA)
                        sy.wait_ge(qGB.h, pp_totB)
                    qV_[j % 2].dma(sy.dma_start(
                        out=guw(j, False),
                        in_=gbf_d.ap()[ib * 128:(ib + 1) * 128, :]))
                    gul_ev[j] = qV_[j % 2].dma(sy.dma_start(
                        out=guw(j, True),
                        in_=ubf_d.ap()[ib * 128:(ib + 1) * 128, :]))

                @block.tensor
                def _(t_, ib=ib, j=j, half=half, phd=prev_half_dve):
                    t_.wait_ge(qV_[j % 2].h, gul_ev[j])
                    if j == 0:
                        t_.wait_ge(eDVE.h, rsx_dve)
                    if half == 1 and j == 0:
                        t_.wait_ge(eDVE.h, yadd_ev[(0, -(-IBH // GD) - 1, TT - 2)])
                    if half == 1 and j == 1:
                        t_.wait_ge(eDVE.h, phd)
                    if j >= 2:
                        t_.wait_ge(eACT.h, gu_act[j - 2])
                        t_.wait_ge(eDVE.h, gu_dve[j - 2])
                        t_.wait_ge(ePO.h, gu_ev[j - 2])
                    PS = P0.ap() if j % 2 == 0 else P1.ap()
                    for (base, up_) in ((0, False), (T, True)):
                        w_ap = guw(j, up_)
                        for hc in range(HC):
                            for tf in range(NTF):
                                t_.matmul(
                                    PS[:, base + tf * 512:base + tf * 512 + 512],
                                    w_ap[:, hc * 128:(hc + 1) * 128],
                                    XT.ap()[:, hc * T + tf * 512:hc * T + tf * 512 + 512],
                                    start=(hc == 0), stop=False)
                        for ri in range(2):
                            rrow = FM.ap()[0:1, ri * T:(ri + 1) * T]
                            for tf in range(NTF):
                                inst = t_.matmul(
                                    PS[:, base + tf * 512:base + tf * 512 + 512],
                                    NEGR, rrow[0:1, tf * 512:tf * 512 + 512],
                                    start=False, stop=(ri == 1))
                    gmm_ev[j] = ePE.inc(inst)

                @block.scalar
                def _(s, ib=ib, j=j, half=half, spl=spill_ev):
                    s.wait_ge(ePE.h, gmm_ev[j])
                    if j == 0 and half == 0:
                        for (sem_, val_) in x1st_fence:
                            s.wait_ge(sem_.h, val_)
                    if j == 0 and half == 1 and spl:
                        s.wait_ge(qOUT.h, spl)
                    if j >= 1:
                        s.wait_ge(ePO.h, h_ev[j - 1])
                    PS = P0.ap() if j % 2 == 0 else P1.ap()
                    gu_act[j] = eACT.inc(s.activation(
                        FSf[:, 0:T], PS[:, 0:T], AF.Sigmoid,
                        scale=SGS.ap()[:, ib:ib + 1]))

                @block.vector
                def _(v, ib=ib, j=j, half=half, php=prev_half_pe):
                    if j >= 1:
                        v.wait_ge(ePO.h, gu_ev[j - 1])
                    if j == 0 and half == 1:
                        v.wait_ge(ePE.h, php)
                    v.wait_ge(ePE.h, gmm_ev[j])
                    vchain(v, v.tensor_scalar(FSf[:, 2 * T:3 * T],
                                              (P0.ap() if j % 2 == 0 else P1.ap())[:, T:2 * T],
                                              SUS.ap()[:, ib:ib + 1], None, ALU.mult))
                    v.wait_ge(eACT.h, gu_act[j])
                    gu_dve[j] = eDVE.inc(v.tensor_scalar(
                        FSf[:, T:2 * T],
                        (P0.ap() if j % 2 == 0 else P1.ap())[:, 0:T],
                        SGS.ap()[:, ib:ib + 1], None, ALU.mult))

                @block.gpsimd
                def _(g, j=j, half=half, php=prev_half_pe):
                    g.wait_ge(eDVE.h, gu_dve[j])
                    if j == 0 and half == 1:
                        g.wait_ge(ePE.h, php)
                    gu_ev[j] = ePO.inc(g.tensor_tensor(
                        FSf[:, 3 * T:4 * T], FSf[:, T:2 * T],
                        FSf[:, 2 * T:3 * T], ALU.mult))
                    g.wait_ge(ePO.h, ePO.n)
                    h_ev[j] = ePO.inc(g.tensor_tensor(hslot(j), FSf[:, 3 * T:4 * T],
                                                      FSf[:, 0:T], ALU.mult))
                    if j >= IBH - 4:
                        jd = j - (IBH - 4)
                        sem = dsems[jd % 4]
                        if jd == 0 and half == 0:
                            g.wait_ge(qGA.h, pp_totA)
                            g.wait_ge(qGB.h, pp_totB)
                        draw_ev[jd] = sem.dma(g.dma_start(
                            out=dwslot(jd),
                            in_=dbf_d.ap()[(half * IBH + jd) * 128:
                                           (half * IBH + jd + 1) * 128, :]))
                        g.wait_ge(sem.h, draw_ev[jd])
                        if jd == 0:
                            g.wait_ge(qMG.h, sdsb_ev)
                        scale_ev[jd] = ePO.inc(g.tensor_tensor(
                            dwslot(jd), dwslot(jd), SDSB, ALU.mult))

            @block.tensor
            def _(t_):
                t_.wait_ge(eACT.h, gu_act[IBH - 2])
                t_.wait_ge(eDVE.h, gu_dve[IBH - 2])
                for j in range(IBH):
                    t_.wait_ge(ePO.h, h_ev[j])
                    for tf in range(NTF):
                        inst = t_.matmul(P0.ap()[32:33, tf * 512:tf * 512 + 512],
                                         ONEB.ap(), hslot(j)[:, tf * 512:tf * 512 + 512],
                                         start=(j == 0), stop=(j == IBH - 1),
                                         tile_position=(0, 32))
                ePE.inc(inst)
            rsh_pe = ePE.n

            @block.vector
            def _(v, rsh_pe=rsh_pe):
                v.wait_ge(ePE.h, rsh_pe)
                vchain(v, v.tensor_scalar(FSf[32:33, 0:T], P0.ap()[32:33, 0:T],
                                          -128.0, None, ALU.mult))
                vchain(v, v.tensor_copy(FM.ap()[32:33, 0:T], FSf[32:33, 0:T]))
                vchain(v, v.tensor_sub(FM.ap()[32:33, T:2 * T], FSf[32:33, 0:T],
                                       FM.ap()[32:33, 0:T]))
            rsh_dve = eDVE.n

            yin_ev = {}
            if half == 0:
                @block.sync
                def _(sy, rshd=rsh_dve):
                    sy.wait_ge(eDVE.h, rshd)
                    for tb in range(TT):
                        sem = yd_sems[tb]
                        sy.wait_ge(sem.h, sem.n)
                        yin_ev[tb] = sem.dma(sy.dma_start(
                            out=yacc(tb),
                            in_=x1_d.ap()[tb * 128:(tb + 1) * 128, :]))
            else:
                if TT > 6:
                    @block.sync
                    def _(sy, rshd=rsh_dve, spl=spill_ev):
                        sy.wait_ge(eDVE.h, rshd)
                        sy.wait_ge(qOUT.h, spl)
                        for tb in (6, 7):
                            sem = yd_sems[tb]
                            sy.wait_ge(sem.h, sem.n)
                            yin_ev[tb] = sem.dma(sy.dma_start(
                                out=yacc(tb),
                                in_=y67_d.ap()[(tb - 6) * 128:(tb - 5) * 128, :]))

            groups = [list(range(i, min(i + GD, IBH))) for i in range(0, IBH, GD)]
            ngr = len(groups)
            dmm_ev = {}

            for j in range(4, min(IBH, 8)):
                @block.gpsimd
                def _(g, j=j, half=half):
                    sem = dsems[j % 4]
                    if j == 4:
                        g.wait_ge(ePE.h, gmm_ev[IBH - 1])
                    draw_ev[j] = sem.dma(g.dma_start(
                        out=dwslot(j),
                        in_=dbf_d.ap()[(half * IBH + j) * 128:
                                       (half * IBH + j + 1) * 128, :]))
                    g.wait_ge(sem.h, draw_ev[j])
                    scale_ev[j] = ePO.inc(g.tensor_tensor(
                        dwslot(j), dwslot(j), SDSB, ALU.mult))

            for gi, grp5 in enumerate(groups):
                for j in grp5:
                    if j < 8:
                        continue
                    @block.gpsimd
                    def _(g, j=j, half=half):
                        sem = dsems[j % 4]
                        g.wait_ge(ePE.h, dmm_ev[((j - 8) // GD, TT - 1)])
                        draw_ev[j] = sem.dma(g.dma_start(
                            out=dwslot(j),
                            in_=dbf_d.ap()[(half * IBH + j) * 128:
                                           (half * IBH + j + 1) * 128, :]))
                        g.wait_ge(sem.h, draw_ev[j])
                        scale_ev[j] = ePO.inc(g.tensor_tensor(
                            dwslot(j), dwslot(j), SDSB, ALU.mult))

                for tb in range(TT):
                    @block.tensor
                    def _(t_, gi=gi, grp5=grp5, tb=tb, rshd=rsh_dve, half=half):
                        t_.wait_ge(ePO.h, scale_ev[grp5[-1]])
                        idx = gi * TT + tb
                        ps = P0.ap() if idx % 2 == 0 else P1.ap()
                        if idx < 2:
                            t_.wait_ge(eDVE.h, rshd)
                        if idx >= 2:
                            kg, ktb = divmod(idx - 2, TT)
                            t_.wait_ge(eDVE.h, yadd_ev[(half, kg, ktb)])
                        elif half == 1:
                            t_.wait_ge(eDVE.h, yadd_ev[(0, ngr - 1, TT - 2 + idx)])
                        last = (gi == ngr - 1)
                        ng5 = len(grp5)
                        for i_loc, j in enumerate(grp5):
                            for fq in range(FQN):
                                inst = t_.matmul(
                                    ps[:, fq * 512:(fq + 1) * 512],
                                    hslot(j)[:, tb * 128:(tb + 1) * 128],
                                    dwslot(j)[:, fq * 512:fq * 512 + 512],
                                    start=(i_loc == 0),
                                    stop=(not last and i_loc == ng5 - 1 and fq == FQN - 1))
                        if last:
                            for ri in range(2):
                                rrow = FM.ap()[32:33, ri * T + tb * 128:ri * T + (tb + 1) * 128]
                                for fq in range(FQN):
                                    inst = t_.matmul(
                                        ps[:, fq * 512:(fq + 1) * 512], rrow,
                                        SDSB[32:33, fq * 512:(fq + 1) * 512],
                                        start=False,
                                        stop=(ri == 1 and fq == FQN - 1),
                                        tile_position=(32, 0))
                        dmm_ev[(gi, tb)] = ePE.inc(inst)

                    @block.vector
                    def _(v, gi=gi, tb=tb, half=half):
                        v.wait_ge(ePE.h, dmm_ev[(gi, tb)])
                        if gi == 0 and (half == 0 or (TT > 6 and tb in (6, 7))):
                            v.wait_ge(yd_sems[tb].h, yin_ev[tb])
                        idx = gi * TT + tb
                        ps = P0.ap() if idx % 2 == 0 else P1.ap()
                        yadd_ev[(half, gi, tb)] = eDVE.inc(v.tensor_tensor(
                            yacc(tb), ps[:, 0:H], yacc(tb), ALU.add))

            if half == 0 and TT > 6:
                @block.sync
                def _(sy, ngr=ngr):
                    for tb in (6, 7):
                        sy.wait_ge(eDVE.h, yadd_ev[(0, ngr - 1, tb)])
                    qOUT.dma(sy.dma_start(out=y67_d.ap()[0:128, :], in_=FSf[:, 0:H]))
                    qOUT.dma(sy.dma_start(out=y67_d.ap()[128:256, :], in_=FSf[:, H:2 * H]))
                spill_ev = qOUT.n
            prev_half_pe = ePE.n
            prev_half_dve = eDVE.n
            prev_half_d6 = dmm_ev[(max(ngr - 2, 0), TT - 1)]

        ngr2 = (IBH + GD - 1) // GD

        @block.sync
        def _(sy, ngr2=ngr2):
            for tb in range(TT):
                sy.wait_ge(eDVE.h, yadd_ev[(1, ngr2 - 1, tb)])
                qOUT.dma(sy.dma_start(
                    out=out_d.ap()[tb * 128:(tb + 1) * 128, :], in_=yacc(tb)))

        @block.gpsimd
        def _(g):
            for s_ in [qLD, qSA, qSB, qOUT, qGA, qGB, qMG, qD0, qD1, qP0, qP1, qPP,
                       qK_[0][0], qK_[0][1], qK_[1][0], qK_[1][1],
                       qV_[0], qV_[1], qE_[0], qE_[1], qE_[2], qE_[3]]:
                g.wait_ge(s_.h, s_.n)

    return nc


def _tile4(w):
    n0, n1 = w.shape
    a = w.reshape(n0 // 128, 128, n1 // 128, 128).transpose(0, 3, 2, 1)
    return np.ascontiguousarray(a).reshape(n0 // 128, 128, n1)


def prep_inputs(cfg, inputs):
    S = cfg["S"]; HEADS = cfg["HEADS"]; I = cfg["I"]
    H = 64 * HEADS
    T = S // NCORES
    IB = I // 128
    x = np.ascontiguousarray(np.asarray(inputs["L0_attn_hidden_states"],
                                        dtype=np.float32).reshape(S, H))
    wq = _tile4(np.asarray(inputs["L0_attn_wq"], dtype=np.float32))
    wk = _tile4(np.asarray(inputs["L0_attn_wk"], dtype=np.float32))
    wv = _tile4(np.asarray(inputs["L0_attn_wv"], dtype=np.float32))
    wo2 = np.ascontiguousarray(np.asarray(inputs["L0_attn_wo"], dtype=np.float32).T)
    relp = np.ascontiguousarray(np.asarray(inputs["L0_attn_rel_pos"], dtype=np.float32))
    gt = _tile4(np.asarray(inputs["L0_ffn_gate_w"], dtype=np.int32))
    ut = _tile4(np.asarray(inputs["L0_ffn_up_w"], dtype=np.int32))
    dt_ = np.ascontiguousarray(
        np.asarray(inputs["L0_ffn_down_w"], dtype=np.int32).T).reshape(IB, 128, H)
    gs = np.ascontiguousarray(
        np.asarray(inputs["L0_ffn_gate_scale"], dtype=np.float32).reshape(IB, 128).T)
    us = np.ascontiguousarray(
        np.asarray(inputs["L0_ffn_up_scale"], dtype=np.float32).reshape(IB, 128).T)
    ds = np.ascontiguousarray(
        np.asarray(inputs["L0_ffn_down_scale"], dtype=np.float32).reshape(1, H))
    ident = np.eye(128).astype(ml_dtypes.bfloat16)
    anti = np.ascontiguousarray(np.eye(128)[::-1]).astype(ml_dtypes.bfloat16)

    in_maps = []
    for c in range(NCORES):
        sel = np.zeros((8, 128), np.float32)
        sel[c, :] = 1.0
        in_maps.append({
            "x": np.ascontiguousarray(x[c * T:(c + 1) * T]),
            "wq_t": wq, "wk_t": wk, "wv_t": wv, "wo_t2": wo2,
            "rel_pos": relp,
            "gate_t": gt, "up_t": ut, "down_t": dt_,
            "gate_s": gs, "up_s": us, "down_s": ds,
            "ident": ident, "anti": anti,
            "sel": sel.astype(ml_dtypes.bfloat16),
        })
    return in_maps


def kernel(**inputs):
    cfg = FULL_CFG
    nc = build(cfg)
    in_maps = prep_inputs(cfg, inputs)
    res = run_bass_kernel_spmd(nc, in_maps, core_ids=list(range(NCORES)))
    S = cfg["S"]; H = 64 * cfg["HEADS"]
    out = np.concatenate([r["out"] for r in res.results], axis=0)
    return out.reshape(1, S, H).astype(np.float32)


# version 17 (final)
# speedup vs baseline: 1.0286x; 1.0007x over previous
import numpy as np
import ml_dtypes
from contextlib import ExitStack

import concourse.bass as bass
import concourse.mybir as mybir
import concourse.bass_utils as _bass_utils
from concourse.bass_utils import run_bass_kernel_spmd

if not getattr(_bass_utils, "_ldw_opt_patched", False):
    _orig_run_command = _bass_utils.run_command

    def _run_command_ldw(argv, **kwargs):
        argv = ["--enable-ldw-opt=true" if a == "--enable-ldw-opt=false" else a
                for a in argv]
        return _orig_run_command(argv, **kwargs)

    _bass_utils.run_command = _run_command_ldw
    _bass_utils._ldw_opt_patched = True

F32 = mybir.dt.float32
BF16 = mybir.dt.bfloat16
I32 = mybir.dt.int32
AF = mybir.ActivationFunctionType
ALU = mybir.AluOpType
AX = mybir.AxisListType

NCORES = 8
QF = 512
EPS = 1e-5

FULL_CFG = dict(S=8192, HEADS=32, I=8192)


class Sem:
    def __init__(self, ctx, nc, name):
        self.h = ctx.enter_context(nc.semaphore(name))
        self.n = 0

    def inc(self, inst, k=1):
        inst.then_inc(self.h, k)
        self.n += k
        return self.n

    def dma(self, inst):
        return self.inc(inst, 16)


def build(cfg):
    S = cfg["S"]; HEADS = cfg["HEADS"]; I = cfg["I"]
    H = 64 * HEADS
    T = S // NCORES
    FB = H // 128
    HC = H // 128
    KC = S // 128
    KCQ = KC // 4
    SQ = S // 4
    QH = T // QF
    NG = HEADS // 4
    TT = T // 128
    IB = I // 128
    ELEN = 2 * S - 1
    WLEN = S + T - 1
    ROWP = WLEN + 129
    GD = 4
    NW = (WLEN + 511) // 512
    FQN = H // 512
    NB4 = FB // 4
    NTH2 = T // 512
    assert T % QF == 0 and KC % 4 == 0 and H % 512 == 0 and FB % 4 == 0

    nc = bass.Bass("TRN2", target_bir_lowering=False)

    x_in = nc.dram_tensor("x", [T, H], F32, kind="ExternalInput")
    wq_t = nc.dram_tensor("wq_t", [FB, 128, H], F32, kind="ExternalInput")
    wk_t = nc.dram_tensor("wk_t", [FB, 128, H], F32, kind="ExternalInput")
    wv_t = nc.dram_tensor("wv_t", [FB, 128, H], F32, kind="ExternalInput")
    wo_t2 = nc.dram_tensor("wo_t2", [H, H], F32, kind="ExternalInput")
    rel = nc.dram_tensor("rel_pos", [1023, 64], F32, kind="ExternalInput")
    gate_t = nc.dram_tensor("gate_t", [IB, 128, H], I32, kind="ExternalInput")
    up_t = nc.dram_tensor("up_t", [IB, 128, H], I32, kind="ExternalInput")
    down_t = nc.dram_tensor("down_t", [IB, 128, H], I32, kind="ExternalInput")
    gate_s = nc.dram_tensor("gate_s", [128, IB], F32, kind="ExternalInput")
    up_s = nc.dram_tensor("up_s", [128, IB], F32, kind="ExternalInput")
    down_s = nc.dram_tensor("down_s", [1, H], F32, kind="ExternalInput")
    ident_in = nc.dram_tensor("ident", [128, 128], BF16, kind="ExternalInput")
    anti_in = nc.dram_tensor("anti", [128, 128], BF16, kind="ExternalInput")
    sel_in = nc.dram_tensor("sel", [8, 128], BF16, kind="ExternalInput")

    out_d = nc.dram_tensor("out", [T, H], F32, kind="ExternalOutput")

    kc_d = nc.dram_tensor("kc_d", [H, T], BF16)
    vc_d = nc.dram_tensor("vc_d", [T, 65 * HEADS], BF16)
    kg_d = nc.dram_tensor("kg_d", [8 * H, T], BF16, addr_space="Shared")
    vg_d = nc.dram_tensor("vg_d", [S, 65 * HEADS], BF16, addr_space="Shared")
    ef_d = nc.dram_tensor("ef_d", [ELEN + 1], F32)
    esh_d = nc.dram_tensor("esh_d", [128 * (ROWP + 1)], BF16)
    x1_d = nc.dram_tensor("x1_d", [T, H], F32)
    sc_d = nc.dram_tensor("sc_d", [4], F32)
    y67_d = nc.dram_tensor("y67_d", [256, H], F32)
    gbf_d = nc.dram_tensor("gbf_d", [IB * 128, H], BF16)
    ubf_d = nc.dram_tensor("ubf_d", [IB * 128, H], BF16)
    dbf_d = nc.dram_tensor("dbf_d", [IB * 128, H], BF16)
    wobf_d = nc.dram_tensor("wobf_d", [H, H], BF16)

    with ExitStack() as ctx:
        e = ctx.enter_context

        QT_C = max(FB * T, min(IB, 32) * 512, min(HC, 8) * H, WLEN + 1)
        CT_C = max(FB * T, max(0, IB - 32) * 512, WLEN + 1)
        XT_C = max(FB * T, 2 * KCQ * 260)
        KB_C = max(4 * SQ, 4 * H, 2 * H)
        WB_C = (2 * GD + 1) * H
        EX_C = max(4 * QF, H)
        AP_C = max(8 * QF, 65 * HEADS + T, 2 * H)

        QT = e(nc.sbuf_tensor("QT", [128, QT_C], BF16))
        CT = e(nc.sbuf_tensor("CT", [128, CT_C], BF16))
        XT = e(nc.sbuf_tensor("XT", [128, XT_C], BF16))
        KB = e(nc.sbuf_tensor("KB", [128, KB_C], BF16))
        WB = e(nc.sbuf_tensor("WB", [128, WB_C], BF16))
        EX = e(nc.sbuf_tensor("EX", [128, EX_C], BF16))
        APB = e(nc.sbuf_tensor("APB", [128, AP_C], BF16))
        ATB = e(nc.sbuf_tensor("ATB", [128, AP_C], BF16))
        XF = e(nc.sbuf_tensor("XF", [128, 2 * H], F32))
        FS = e(nc.sbuf_tensor("FS", [128, 4 * T], F32))
        FM = e(nc.sbuf_tensor("FM", [128, 2048], BF16))
        SID = e(nc.sbuf_tensor("SID", [128, 128], BF16))
        SAN = e(nc.sbuf_tensor("SAN", [128, 128], BF16))
        SSEL = e(nc.sbuf_tensor("SSEL", [8, 128], BF16))
        SREL = e(nc.sbuf_tensor("SREL", [128, 512], F32))
        SBV = e(nc.sbuf_tensor("SBV", [128, 8], F32))
        SBVB = e(nc.sbuf_tensor("SBVB", [128, 8], BF16))
        SGS = e(nc.sbuf_tensor("SGS", [128, IB], F32))
        SUS = e(nc.sbuf_tensor("SUS", [128, IB], F32))
        SCN = e(nc.sbuf_tensor("SCN", [128, 16], F32))
        ONEB = e(nc.sbuf_tensor("ONEB", [128, 1], BF16))
        ONEF = e(nc.sbuf_tensor("ONEF", [65, 64], F32))
        NEGR = ONEF.ap()[0:1, :].bitcast(BF16)

        P0 = e(nc.psum_tensor("P0", [128, 2048], F32))
        P1 = e(nc.psum_tensor("P1", [128, 2048], F32))

        qLD = Sem(ctx, nc, "qLD")
        qSA = Sem(ctx, nc, "qSA")
        qSB = Sem(ctx, nc, "qSB")
        qOUT = Sem(ctx, nc, "qOUT")
        qK_ = [[Sem(ctx, nc, f"qK{a}{b}") for b in range(2)] for a in range(2)]
        qV_ = [Sem(ctx, nc, f"qV{b}") for b in range(2)]
        qE_ = [Sem(ctx, nc, f"qE{b}") for b in range(4)]
        qGA = Sem(ctx, nc, "qGA")
        qGB = Sem(ctx, nc, "qGB")
        qMG = Sem(ctx, nc, "qMG")
        qD0 = Sem(ctx, nc, "qD0")
        qD1 = Sem(ctx, nc, "qD1")
        qPP = Sem(ctx, nc, "qPP")
        qWO = Sem(ctx, nc, "qWO")
        qP0 = Sem(ctx, nc, "qP0")
        qP1 = Sem(ctx, nc, "qP1")
        qP_ = [qP0, qP1]
        qS_ = [qSA, qSB]
        qG_ = [qGA, qGB]
        ePE = Sem(ctx, nc, "ePE")
        eACT = Sem(ctx, nc, "eACT")
        eDVE = Sem(ctx, nc, "eDVE")
        ePO = Sem(ctx, nc, "ePO")
        sCC = Sem(ctx, nc, "sCC")

        block = e(nc.Block())

        FSf = FS.ap()
        XFf = XF.ap()
        FMf = FM.ap()[:, 0:2048].bitcast(F32)
        P1b = P1.ap()[:, 0:512].bitcast(BF16)[:, 0:512]

        def xslot(i):
            return XFf[:, (i % 2) * H:((i % 2) * H + H)]

        def xlnslot(i):
            return APB.ap()[:, (i % 2) * H:((i % 2) * H + H)]

        def yacc(tb):
            if tb == 0:
                return APB.ap()[:, 0:2 * H].bitcast(F32)[:, 0:H]
            if tb == 1:
                return ATB.ap()[:, 0:2 * H].bitcast(F32)[:, 0:H]
            if tb in (2, 3):
                return XFf[:, (tb - 2) * H:((tb - 2) * H + H)]
            if tb in (4, 5):
                return KB.ap()[:, 0:4 * H].bitcast(F32)[:, (tb - 4) * H:(tb - 3) * H]
            return FSf[:, (tb - 6) * H:(tb - 5) * H]

        def hchunk(ib):
            if ib < 32:
                return QT.ap()[:, ib * 512:(ib + 1) * 512]
            return CT.ap()[:, (ib - 32) * 512:(ib - 31) * 512]

        def woslot(hc):
            if hc < min(HC, 8):
                return QT.ap()[:, hc * H:(hc + 1) * H]
            if hc < 12:
                return KB.ap()[:, (hc - 8) * H:(hc - 7) * H]
            return WB.ap()[:, (hc - 12) * H:(hc - 11) * H]

        def vchain(v, inst):
            eDVE.inc(inst)
            v.wait_ge(eDVE.h, eDVE.n)

        def newton_rsqrt(v):
            vv = SCN.ap()[:, 8:9]
            vh = SCN.ap()[:, 12:13]
            yy = SCN.ap()[:, 9:10]
            t1 = SCN.ap()[:, 13:14]
            vchain(v, v.tensor_scalar_mul(vh, vv, 0.5))
            b = vv.bitcast(I32)
            yb = yy.bitcast(I32)
            vchain(v, v.tensor_single_scalar(yb, b, 1, ALU.logical_shift_right))
            vchain(v, v.tensor_tensor(yb, SCN.ap()[:, 15:16].bitcast(I32), yb, ALU.subtract))
            for _ in range(3):
                vchain(v, v.tensor_tensor(t1, yy, yy, ALU.mult))
                vchain(v, v.tensor_tensor(t1, t1, vh, ALU.mult))
                vchain(v, v.tensor_scalar(t1, t1, -1.0, 1.5, ALU.mult, ALU.add))
                vchain(v, v.tensor_tensor(yy, yy, t1, ALU.mult))

        def emit_ln(v, src, dst_bf16, scratch=None):
            if scratch is None:
                scratch = P0.ap()[:, 0:H]
            mean = SCN.ap()[:, 0:1]
            ssq = SCN.ap()[:, 1:2]
            vchain(v, v.reduce_sum(mean, src, axis=AX.X))
            vchain(v, v.tensor_scalar_mul(mean, mean, 1.0 / H))
            vchain(v, v.scalar_tensor_tensor(scratch, src, mean, src,
                                             ALU.subtract, ALU.mult, accum_out=ssq))
            vchain(v, v.tensor_scalar(SCN.ap()[:, 8:9], ssq, 1.0 / H, EPS,
                                      ALU.mult, ALU.add))
            newton_rsqrt(v)
            vchain(v, v.tensor_scalar(dst_bf16, src, mean, SCN.ap()[:, 9:10],
                                      ALU.subtract, ALU.mult))

        @block.gpsimd
        def _(g):
            g.memset(SREL.ap(), 0.0)
            g.memset(ONEB.ap(), 1.0)
            g.memset(NEGR, -128.0)
            ePO.inc(g.memset(ONEF.ap()[64:65, :], 1.0))
        po_consts = ePO.n

        @block.sync
        def _(sy):
            qLD.dma(sy.dma_start(out=SID.ap(), in_=ident_in.ap()))
            qLD.dma(sy.dma_start(out=SAN.ap(), in_=anti_in.ap()))
            qLD.dma(sy.dma_start(out=SSEL.ap(), in_=sel_in.ap()))
            qLD.dma(sy.dma_start(out=SGS.ap(), in_=gate_s.ap()))
            qLD.dma(sy.dma_start(out=SUS.ap(), in_=up_s.ap()))
            sy.wait_ge(ePO.h, po_consts)
            qLD.dma(sy.dma_start(
                out=SREL.ap()[:, 0:448].rearrange("p (g d) -> p g d", g=7),
                in_=rel.ap()[0:896, :].rearrange("(g p) d -> p g d", p=128)))
            qLD.dma(sy.dma_start(out=SREL.ap()[0:127, 448:512],
                                 in_=rel.ap()[896:1023, :]))
        ld_setup = qLD.n

        @block.vector
        def _(v):
            v.wait_ge(qLD.h, ld_setup)
            for g8 in range(8):
                eDVE.inc(v.reduce_sum(SBV.ap()[:, g8:g8 + 1],
                                      SREL.ap()[:, g8 * 64:(g8 + 1) * 64], axis=AX.X))
            v.wait_ge(eDVE.h, eDVE.n)
            vchain(v, v.tensor_scalar_mul(SBV.ap(), SBV.ap(), 1.0 / 64.0))
            vchain(v, v.tensor_copy(SBVB.ap(), SBV.ap()))
            vchain(v, v.memset(SCN.ap()[:, 15:16].bitcast(I32), 0x5F3759DF))
        dv_bv = eDVE.n

        @block.tensor
        def _(t):
            t.wait_ge(eDVE.h, dv_bv)
            for g8 in range(8):
                inst = t.matmul(P1.ap()[0:1, (7 - g8) * 128:(8 - g8) * 128],
                                SBVB.ap()[:, g8:g8 + 1], SAN.ap(),
                                start=True, stop=True)
            ePE.inc(inst)
        pe_rev = ePE.n

        @block.scalar
        def _(s):
            s.wait_ge(ePE.h, pe_rev)
            eACT.inc(s.copy(FSf[0:1, 0:1024], P1.ap()[0:1, 0:1024]))
        act_rev = eACT.n

        RE = (ELEN + 1) // 1024
        r1, c1 = divmod(S - 512, 1024)

        @block.sync
        def _(sy):
            sy.wait_ge(eACT.h, act_rev)
            qLD.dma(sy.dma_start(out=sc_d.ap()[0:1], in_=FSf[0:1, 1:2],
                                 allow_slow_non_contiguous=True))
            qLD.dma(sy.dma_start(out=sc_d.ap()[1:2], in_=FSf[0:1, 1023:1024],
                                 allow_slow_non_contiguous=True))
            sy.wait_ge(qLD.h, qLD.n)
            qLD.dma(sy.dma_start(out=SCN.ap()[0:r1 + 1, 10:11],
                                 in_=bass.AP(sc_d, 0, [[0, r1 + 1], [1, 1]]),
                                 allow_slow_non_contiguous=True))
            qLD.dma(sy.dma_start(out=SCN.ap()[r1 + 1:RE, 10:11],
                                 in_=bass.AP(sc_d, 1, [[0, RE - (r1 + 1)], [1, 1]]),
                                 allow_slow_non_contiguous=True))
        ld_scol = qLD.n

        @block.vector
        def _(v):
            v.wait_ge(qLD.h, ld_scol)
            SEF2 = XFf[0:RE, 0:1024]
            vchain(v, v.memset(SEF2, 0.0))
            vchain(v, v.tensor_scalar_add(SEF2, SEF2, SCN.ap()[0:RE, 10:11]))
        dv_sef = eDVE.n

        @block.sync
        def _(sy):
            sy.wait_ge(eDVE.h, dv_sef)
            pos = 0
            rr, cc = r1, c1
            while pos < 1023:
                seg = min(1023 - pos, 1024 - cc)
                qLD.dma(sy.dma_start(out=XFf[rr:rr + 1, cc:cc + seg],
                                     in_=FSf[0:1, 1 + pos:1 + pos + seg]))
                pos += seg
                rr += 1
                cc = 0
            sy.wait_ge(qLD.h, qLD.n)
            qLD.dma(sy.dma_start(
                out=ef_d.ap().rearrange("(r c) -> r c", c=1024),
                in_=XFf[0:RE, 0:1024]))
        ld_ef = qLD.n

        @block.gpsimd
        def _(g):
            g.wait_ge(qLD.h, ld_ef)
            qMG.dma(g.dma_start(out=QT.ap()[0:8, 0:WLEN],
                                in_=bass.AP(ef_d, 0, [[T, 8], [1, WLEN]])))
        lg_cand = qMG.n

        ppin_ev = {}
        pp_early = []
        for i_ in range(min(8, IB // 2)):
            pp_early += [i_, IB + i_]
        pp_early_set = set(pp_early)
        pp_order = pp_early + [k for k in range(3 * IB) if k not in pp_early_set]
        pp_seg = pp_order

        pe_w0 = ePE.n
        act_w0 = eACT.n

        @block.tensor
        def _(t):
            t.wait_ge(qMG.h, lg_cand)
            t.wait_ge(eACT.h, act_rev)
            for w in range(NW):
                lo = w * 512
                ln = min(512, WLEN - lo)
                if w >= 2:
                    t.wait_ge(eACT.h, act_w0 + (w - 2) + 1)
                ePE.inc(t.matmul(P1.ap()[:, (w % 2) * 512:(w % 2) * 512 + ln],
                                 SSEL.ap(), QT.ap()[0:8, lo:lo + ln],
                                 start=True, stop=True))

        @block.scalar
        def _(s):
            for w in range(NW):
                lo = w * 512
                ln = min(512, WLEN - lo)
                s.wait_ge(ePE.h, pe_w0 + w + 1)
                eACT.inc(s.activation(CT.ap()[:, lo:lo + ln],
                                      P1.ap()[:, (w % 2) * 512:(w % 2) * 512 + ln],
                                      AF.Exp))
        act_wexp = eACT.n

        @block.sync
        def _(sy):
            sy.wait_ge(eACT.h, act_wexp)
            qLD.dma(sy.dma_start(out=bass.AP(esh_d, 0, [[ROWP + 1, 128], [1, WLEN]]),
                                 in_=CT.ap()[:, 0:WLEN]))
        ld_esh = qLD.n

        @block.gpsimd
        def _(g):
            g.wait_ge(qLD.h, ld_esh)
            for hc in range(HC):
                qP0.dma(g.dma_start(
                    out=wobf_d.ap()[hc * 128:(hc + 1) * 128, :],
                    in_=wo_t2.ap()[hc * 128:(hc + 1) * 128, :]))
        wocast_tot = qP0.n

        def emit_transposes(src_tile_fn, dst_col_fn, first_pe_waits):
            copies_base = eDVE.n
            pe_list = []

            @block.tensor
            def _(t_):
                for (sem, val) in first_pe_waits:
                    t_.wait_ge(sem, val)
                for b in range(NB4):
                    if b >= 1:
                        t_.wait_ge(eDVE.h, copies_base + b)
                    for j in range(4):
                        inst = t_.transpose(P1b[:, j * 128:(j + 1) * 128],
                                            src_tile_fn(b * 4 + j), SID.ap())
                    pe_list.append(ePE.inc(inst))

            @block.vector
            def _(v):
                for b in range(NB4):
                    v.wait_ge(ePE.h, pe_list[b])
                    eDVE.inc(v.tensor_copy(
                        dst_col_fn(b),
                        P1b.rearrange("p (f u) -> p f u", f=4)))
            assert eDVE.n == copies_base + NB4
            return pe_list[-1], eDVE.n

        ln_ev = {}
        tr_pe_ev = {}
        x_ev = {}
        ptcopy_last = 0
        for tt in range(TT):
            @block.sync
            def _(sy, tt=tt):
                if tt >= 2:
                    sy.wait_ge(eDVE.h, ln_ev[tt - 2])
                x_ev[tt] = qS_[tt % 2].dma(sy.dma_start(
                    out=xslot(tt), in_=x_in.ap()[tt * 128:(tt + 1) * 128, :]))

            @block.vector
            def _(v, tt=tt):
                v.wait_ge(qS_[tt % 2].h, x_ev[tt])
                if tt >= 2:
                    v.wait_ge(ePE.h, tr_pe_ev[tt - 2])
                emit_ln(v, xslot(tt), xlnslot(tt))
                ln_ev[tt] = eDVE.inc(v.engine_nop())

            waits = [(eDVE.h, ln_ev[tt])]
            if tt == 0:
                waits.append((qLD.h, ld_esh))
            else:
                waits.append((eDVE.h, ptcopy_last))

            def mksrc(tt):
                return lambda fb: xlnslot(tt)[:, fb * 128:(fb + 1) * 128]

            def mkdst(tt):
                return lambda b: (XT.ap()[:, 0:FB * T]
                                  .rearrange("p (f u) -> p f u", f=FB)
                                  [:, b * 4:b * 4 + 4, tt * 128:(tt + 1) * 128])

            tr_pe_ev[tt], ptcopy_last = emit_transposes(mksrc(tt), mkdst(tt), waits)
        ph1_end_dve = eDVE.n

        def emit_proj(wt, evac, first_waits=()):
            mm_ev = {}
            wl_ev = {}
            cast_ev = {}
            ev_done = {}

            def emit_load_cast(fb):
                @block.sync
                def _(sy, fb=fb):
                    if fb >= 2:
                        sy.wait_ge(eACT.h, cast_ev[fb - 2])
                    wl_ev[fb] = qV_[fb % 2].dma(sy.dma_start(
                        out=FSf[:, (fb % 2) * (2 * T):(fb % 2) * (2 * T) + H],
                        in_=wt.ap()[fb]))

                @block.scalar
                def _(s, fb=fb):
                    s.wait_ge(qV_[fb % 2].h, wl_ev[fb])
                    if fb >= 2:
                        s.wait_ge(ePE.h, mm_ev[fb - 2])
                    cast_ev[fb] = eACT.inc(s.copy(
                        WB.ap()[:, (fb % 2) * H:((fb % 2) * H + H)],
                        FSf[:, (fb % 2) * (2 * T):(fb % 2) * (2 * T) + H]))

            def emit_mm(fb):
                @block.tensor
                def _(t_, fb=fb):
                    t_.wait_ge(eACT.h, cast_ev[fb])
                    if fb == 0:
                        for (semh, val) in first_waits:
                            t_.wait_ge(semh, val)
                    if fb >= 1:
                        t_.wait_ge(eACT.h, ev_done[fb - 1])
                    for hc in range(HC):
                        for th2 in range(NTH2):
                            inst = t_.matmul(
                                P0.ap()[:, th2 * 512:(th2 + 1) * 512],
                                WB.ap()[:, (fb % 2) * H + hc * 128:(fb % 2) * H + (hc + 1) * 128],
                                XT.ap()[:, hc * T + th2 * 512:hc * T + th2 * 512 + 512],
                                start=(hc == 0), stop=(hc == HC - 1))
                    mm_ev[fb] = ePE.inc(inst)

            emit_load_cast(0)
            emit_mm(0)
            for fb in range(1, FB):
                emit_load_cast(fb)
                evac(fb - 1, mm_ev[fb - 1])
                ev_done[fb - 1] = eACT.n
                emit_mm(fb)
            evac(FB - 1, mm_ev[FB - 1])
            ev_done[FB - 1] = eACT.n
            return mm_ev

        kst_ev = {}

        def kstg(fb):
            base = 65 * HEADS
            return (APB.ap()[:, base:base + T] if fb % 2 == 0
                    else ATB.ap()[:, base:base + T])

        def kevac(fb, mmv):
            @block.scalar
            def _(s, fb=fb, mmv=mmv):
                s.wait_ge(ePE.h, mmv)
                if fb >= 2:
                    s.wait_ge(qS_[fb % 2].h, kst_ev[fb - 2])
                eACT.inc(s.copy(kstg(fb), P0.ap()[:, 0:NTH2 * 512]))
            a_now = eACT.n

            @block.sync
            def _(sy, fb=fb, a_now=a_now):
                sy.wait_ge(eACT.h, a_now)
                kst_ev[fb] = qS_[fb % 2].dma(sy.dma_start(
                    out=kc_d.ap()[fb * 128:(fb + 1) * 128, :], in_=kstg(fb)))

        def vevac(fb, mmv):
            @block.scalar
            def _(s, fb=fb, mmv=mmv):
                s.wait_ge(ePE.h, mmv)
                if fb == 0:
                    s.wait_ge(qLD.h, ld_esh)
                eACT.inc(s.copy(CT.ap()[:, fb * T:fb * T + T], P0.ap()[:, 0:NTH2 * 512]))

        def qevac(fb, mmv):
            @block.scalar
            def _(s, fb=fb, mmv=mmv):
                s.wait_ge(ePE.h, mmv)
                eACT.inc(s.activation(QT.ap()[:, fb * T:fb * T + T],
                                      P0.ap()[:, 0:NTH2 * 512], AF.Copy, scale=0.125))

        emit_proj(wk_t, kevac, first_waits=[(eDVE.h, ph1_end_dve)])

        kst_totA, kst_totB = qSA.n, qSB.n

        @block.gpsimd
        def _(g):
            g.wait_ge(qSA.h, kst_totA)
            g.wait_ge(qSB.h, kst_totB)
            sCC.inc(g.collective_compute(
                "AllGather", ALU.bypass, replica_groups=[list(range(NCORES))],
                ins=[kc_d.ap()], outs=[kg_d.ap()]))

        emit_proj(wv_t, vevac, first_waits=[(eACT.h, eACT.n)])
        v_all = eACT.n

        vst_ev = {}
        vtp_last = ptcopy_last
        for tb in range(TT):
            vcb = (APB.ap()[:, 0:65 * HEADS] if tb % 2 == 0
                   else ATB.ap()[:, 0:65 * HEADS])

            @block.gpsimd
            def _(g, tb=tb, vcb=vcb):
                if tb >= 2:
                    g.wait_ge(qS_[tb % 2].h, vst_ev[tb - 2])
                ePO.inc(g.memset(vcb, 1.0))
            po_now = ePO.n

            waits = [(eACT.h, v_all), (ePO.h, po_now), (eDVE.h, vtp_last)]

            def mksrcv(tb):
                return lambda fb: CT.ap()[:, fb * T + tb * 128:fb * T + tb * 128 + 128]

            def mkdstv(vcb):
                return lambda b: (vcb.rearrange("p (f u) -> p f u", f=HEADS // 2)
                                  [:, b * 4:b * 4 + 4, :]
                                  .rearrange("p f (g u) -> p f g u", g=2)[:, :, :, 0:64])

            copies_base = eDVE.n
            pe_list = []

            @block.tensor
            def _(t_, tb=tb):
                for (semh, val) in waits:
                    t_.wait_ge(semh, val)
                for b in range(NB4):
                    if b >= 1:
                        t_.wait_ge(eDVE.h, copies_base + b)
                    for j in range(4):
                        inst = t_.transpose(P1b[:, j * 128:(j + 1) * 128],
                                            mksrcv(tb)(b * 4 + j), SID.ap())
                    pe_list.append(ePE.inc(inst))

            @block.vector
            def _(v, tb=tb, vcb=vcb):
                for b in range(NB4):
                    v.wait_ge(ePE.h, pe_list[b])
                    eDVE.inc(v.tensor_copy(
                        mkdstv(vcb)(b),
                        P1b[:, 0:512].rearrange("p (f g u) -> p f g u", f=4, g=2)))
            vtp_last = eDVE.n

            @block.sync
            def _(sy, tb=tb, vcb=vcb, d_now=vtp_last):
                sy.wait_ge(eDVE.h, d_now)
                vst_ev[tb] = qS_[tb % 2].dma(sy.dma_start(
                    out=vc_d.ap()[tb * 128:(tb + 1) * 128, :], in_=vcb))

        vst_totA, vst_totB = qSA.n, qSB.n

        @block.gpsimd
        def _(g):
            g.wait_ge(qSA.h, vst_totA)
            g.wait_ge(qSB.h, vst_totB)
            sCC.inc(g.collective_compute(
                "AllGather", ALU.bypass, replica_groups=[list(range(NCORES))],
                ins=[vc_d.ap()], outs=[vg_d.ap()]))
        cc_all = sCC.n

        emit_proj(wq_t, qevac, first_waits=[(eACT.h, v_all)])
        q_mm_last = ePE.n
        q_all = eACT.n

        s01_evs, s23_evs, av_evs = {}, {}, {}
        eA_evs, eB_evs, m01_evs, m23_evs, eb_evs = {}, {}, {}, {}, {}
        woS_ev = {}

        @block.sync
        def _(sy):
            sy.wait_ge(ePE.h, q_mm_last)
            sy.wait_ge(qP0.h, wocast_tot)
            for hc in range(12, min(HC, 16)):
                woS_ev[hc] = qWO.dma(sy.dma_start(
                    out=woslot(hc), in_=wobf_d.ap()[hc * 128:(hc + 1) * 128, :]))
        NSEG = QH * NG
        SPB = -(-len(pp_seg) // NSEG) if pp_seg else 1
        ctxfree = vtp_last
        p0free = (eACT.h, q_all)
        tg = 0

        for qh in range(QH):
            for grp in range(NG):
                seg0 = tg
                pe_base, act_base, dve_base = ePE.n, eACT.n, eDVE.n
                for kc in range(KC):
                    t = seg0 + kc
                    if kc == 0:
                        s01_evs[t] = pe_base + 1
                        s23_evs[t] = pe_base + 2
                    else:
                        s01_evs[t] = pe_base + 3 * kc
                        s23_evs[t] = pe_base + 3 * kc + 1
                    av_evs[t] = pe_base + (3 * kc + 5 if kc < KC - 1 else 3 * KC)
                    eA_evs[t] = act_base + 2 * kc + 1
                    eB_evs[t] = act_base + 2 * kc + 2
                    m01_evs[t] = dve_base + 2 * kc + 1
                    m23_evs[t] = dve_base + 2 * kc + 2

                kq_ev = {}
                vq_ev = {}

                @block.sync
                def _(sy, grp=grp, seg0=seg0):
                    if seg0 == 0:
                        sy.wait_ge(sCC.h, cc_all)
                        sy.wait_ge(ePE.h, q_mm_last)
                    for q in range(4):
                        if q >= 2:
                            sy.wait_ge(ePE.h, s23_evs[seg0 + (q - 1) * KCQ - 1])
                        elif seg0 > 0:
                            sy.wait_ge(ePE.h, s23_evs[seg0 - 1 - (1 - q) * KCQ])
                        for pi in range(2):
                            pp = 2 * grp + pi
                            kq_ev[(pi, q)] = qK_[pi][q % 2].dma(sy.dma_start(
                                out=KB.ap()[:, (pi * 2 + q % 2) * SQ:(pi * 2 + q % 2) * SQ + SQ]
                                    .rearrange("p (r u) -> p r u", r=2),
                                in_=kg_d.ap().rearrange("(r h) u -> r h u", h=H)
                                    [2 * q:2 * q + 2, pp * 128:(pp + 1) * 128, :]
                                    .transpose([1, 0, 2])))
                        if q >= 2:
                            sy.wait_ge(ePE.h, av_evs[seg0 + (q - 1) * KCQ - 1])
                        elif seg0 > 0:
                            sy.wait_ge(ePE.h, av_evs[seg0 - 1 - (1 - q) * KCQ])
                        vq_ev[q] = qV_[q % 2].dma(sy.dma_start(
                            out=XT.ap()[:, (q % 2) * KCQ * 260:((q % 2) + 1) * KCQ * 260]
                                .rearrange("p (k u) -> p k u", k=KCQ),
                            in_=vg_d.ap().rearrange("(k p) u -> p k u", p=128)
                                [:, q * KCQ:(q + 1) * KCQ, grp * 260:(grp + 1) * 260]))

                si = seg0 // KC
                myk = pp_seg[si * SPB:(si + 1) * SPB]
                kq_sched = {}
                for xi, k in enumerate(myk):
                    kq_sched[min(KC - 1, 2 + xi * max(1, KC // max(1, len(myk))))] = k
                if si == NSEG - 1 and HC > 7 and seg0 > 0:
                    @block.sync
                    def _(sy, wo_prev=s23_evs[seg0 - 1]):
                        sy.wait_ge(ePE.h, wo_prev)
                        for hc in range(0, 7):
                            woS_ev[hc] = qWO.dma(sy.dma_start(
                                out=woslot(hc),
                                in_=wobf_d.ap()[hc * 128:(hc + 1) * 128, :]))

                @block.gpsimd
                def _(g, qh=qh, seg0=seg0, myk=myk, kq_sched=kq_sched):
                    for kc in range(KC):
                        t = seg0 + kc
                        if t == 0:
                            g.wait_ge(qLD.h, ld_esh)
                        if t >= 4:
                            g.wait_ge(eDVE.h, m23_evs[t - 4])
                        off = (S - 1) + 512 * qh - 128 * kc
                        eb_evs[t] = qE_[kc % 4].dma(g.dma_start(
                            out=EX.ap()[:, (kc % 4) * 512:(kc % 4) * 512 + 512],
                            in_=bass.AP(esh_d, off, [[ROWP, 128], [1, 512]])))
                        if kc in kq_sched:
                            k = kq_sched[kc]
                            srcs = (gate_t, up_t, down_t)
                            dsts = (gbf_d, ubf_d, dbf_d)
                            r = k % IB
                            csem = qPP if k in pp_early_set else qG_[k % 2]
                            ppin_ev[k] = csem.dma(g.dma_start(
                                out=dsts[k // IB].ap()[r * 128:(r + 1) * 128, :],
                                in_=srcs[k // IB].ap()[k % IB]))

                @block.tensor
                def _(t_, qh=qh, grp=grp, seg0=seg0, p0free=p0free, ctxfree=ctxfree):
                    for kc in range(KC):
                        t = seg0 + kc
                        q = kc // KCQ
                        kq = kc % KCQ
                        if kq == 0:
                            for pi in range(2):
                                t_.wait_ge(qK_[pi][q % 2].h, kq_ev[(pi, q)])
                        for half2 in range(2):
                            if kc == 0:
                                if half2 == 0:
                                    t_.wait_ge(*p0free)
                            else:
                                t_.wait_ge(eACT.h, (eA_evs if half2 == 0 else eB_evs)[t - 1])
                            for j in (0, 1) if half2 == 0 else (2, 3):
                                pi, sg = j // 2, j % 2
                                ks = (pi * 2 + q % 2) * SQ
                                inst = t_.matmul(
                                    P0.ap()[:, j * 512:(j + 1) * 512],
                                    KB.ap()[sg * 64:(sg + 1) * 64,
                                            ks + kq * 128:ks + (kq + 1) * 128],
                                    QT.ap()[sg * 64:(sg + 1) * 64,
                                            (2 * grp + pi) * T + qh * 512:(2 * grp + pi) * T + qh * 512 + 512],
                                    start=True, stop=True, tile_position=(sg * 64, 0))
                            assert ePE.inc(inst) == (s01_evs if half2 == 0 else s23_evs)[t]
                        kcps = [kc - 1, kc] if kc == KC - 1 else ([kc - 1] if kc >= 1 else [])
                        for kcp in kcps:
                            tp = seg0 + kcp
                            qp = kcp // KCQ
                            kqp = kcp % KCQ
                            if kqp == 0:
                                t_.wait_ge(qV_[qp % 2].h, vq_ev[qp])
                            if kcp == 0:
                                t_.wait_ge(eDVE.h, ctxfree)
                            for j in range(4):
                                if j == 0:
                                    t_.wait_ge(eDVE.h, m01_evs[tp])
                                if j == 2:
                                    t_.wait_ge(eDVE.h, m23_evs[tp])
                                vb = (qp % 2) * KCQ * 260 + kqp * 260 + j * 65
                                inst = t_.matmul(
                                    P1.ap()[0:65, j * 512:(j + 1) * 512],
                                    XT.ap()[:, vb:vb + 65],
                                    ATB.ap()[:, (tp % 2) * 2048 + j * 512:(tp % 2) * 2048 + (j + 1) * 512],
                                    start=(kcp == 0), stop=(kcp == KC - 1))
                            assert ePE.inc(inst) == av_evs[tp]

                @block.scalar
                def _(s, seg0=seg0):
                    for kc in range(KC):
                        t = seg0 + kc
                        s.wait_ge(ePE.h, s01_evs[t])
                        if t >= 2:
                            s.wait_ge(eDVE.h, m01_evs[t - 2])
                        assert eACT.inc(s.activation(
                            APB.ap()[:, (t % 2) * 2048:(t % 2) * 2048 + 1024],
                            P0.ap()[:, 0:1024], AF.Exp)) == eA_evs[t]
                        s.wait_ge(ePE.h, s23_evs[t])
                        if t >= 2:
                            s.wait_ge(eDVE.h, m23_evs[t - 2])
                        assert eACT.inc(s.activation(
                            APB.ap()[:, (t % 2) * 2048 + 1024:(t % 2) * 2048 + 2048],
                            P0.ap()[:, 1024:2048], AF.Exp)) == eB_evs[t]

                @block.vector
                def _(v, seg0=seg0):
                    for kc in range(KC):
                        t = seg0 + kc
                        v.wait_ge(qE_[kc % 4].h, eb_evs[t])
                        ebrep = bass.AP(EX.ap().tensor, (kc % 4) * 512,
                                        [[EX_C, 128], [0, 2], [1, 512]])
                        for half2 in range(2):
                            v.wait_ge(eACT.h, (eA_evs if half2 == 0 else eB_evs)[t])
                            if t >= 2:
                                v.wait_ge(ePE.h, av_evs[t - 2])
                            base = (t % 2) * 2048 + half2 * 1024
                            inst = v.tensor_tensor(
                                ATB.ap()[:, base:base + 1024].rearrange("p (r u) -> p r u", r=2),
                                APB.ap()[:, base:base + 1024].rearrange("p (r u) -> p r u", r=2),
                                ebrep, ALU.mult)
                            assert eDVE.inc(inst) == (m01_evs if half2 == 0 else m23_evs)[t]

                tlast = seg0 + KC - 1

                rec_base = eDVE.n

                @block.vector
                def _(v, tlast=tlast):
                    v.wait_ge(ePE.h, av_evs[tlast])
                    for j in range(4):
                        eDVE.inc(v.reciprocal(FSf[64:65, j * 512:(j + 1) * 512],
                                              P1.ap()[64:65, j * 512:(j + 1) * 512]))

                @block.tensor
                def _(t_, rb=rec_base, tlast=tlast):
                    t_.wait_ge(eACT.h, eB_evs[tlast])
                    for j in range(4):
                        t_.wait_ge(eDVE.h, rb + j + 1)
                        inst = t_.matmul(P0.ap()[0:64, j * 512:(j + 1) * 512],
                                         ONEF.ap()[64:65, :],
                                         FSf[64:65, j * 512:(j + 1) * 512],
                                         start=True, stop=True, tile_position=(64, 0))
                        ePE.inc(inst)
                bc_ev = ePE.n
                assert bc_ev == pe_base + 3 * KC + 4

                @block.vector
                def _(v, bc_ev=bc_ev, qh=qh, grp=grp):
                    v.wait_ge(ePE.h, bc_ev)
                    vchain(v, v.tensor_copy(FSf[0:64, 0:2048], P0.ap()[0:64, :]))
                    for j in range(4):
                        pi, sg = j // 2, j % 2
                        dst = CT.ap()[sg * 64:(sg + 1) * 64,
                                      (2 * grp + pi) * T + qh * 512:(2 * grp + pi) * T + qh * 512 + 512]
                        inst = v.tensor_tensor(dst, P1.ap()[0:64, j * 512:(j + 1) * 512],
                                               FSf[0:64, j * 512:(j + 1) * 512], ALU.mult)
                    eDVE.inc(inst)
                ctxfree = eDVE.n
                assert ctxfree == dve_base + 2 * KC + 6
                p0free = (eDVE.h, dve_base + 2 * KC + 5)
                tg += KC

        assert len(ppin_ev) == 3 * IB, (len(ppin_ev), 3 * IB)
        pp_totA, pp_totB = qGA.n, qGB.n
        qpp_tot = qPP.n
        last_s = s23_evs[tg - 1]

        wo_ev = {}

        @block.sync
        def _(sy):
            sy.wait_ge(ePE.h, last_s)
            sy.wait_ge(qP0.h, wocast_tot)
            order = [h for h in (8, 9, 10, 11, 7) if h < HC] + \
                    [h for h in range(HC) if h not in (7, 8, 9, 10, 11)]
            for hc in order:
                if hc in woS_ev:
                    continue
                woS_ev[hc] = qWO.dma(sy.dma_start(
                    out=woslot(hc), in_=wobf_d.ap()[hc * 128:(hc + 1) * 128, :]))

        op_ev = {}
        x1_ev = {}
        x1add_ev = {}
        x1st_ev = {}
        xr_ev = {}
        x1t_pe_ev = {}
        ptcopy_last4 = None

        def mksrc4(tb):
            return lambda fb: xlnslot(tb)[:, fb * 128:(fb + 1) * 128]

        def mkdst4(tb):
            return lambda b: (XT.ap()[:, 0:FB * T]
                              .rearrange("p (f u) -> p f u", f=FB)
                              [:, b * 4:b * 4 + 4, tb * 128:(tb + 1) * 128])

        def emit_x1t(tp):
            waits = [(eDVE.h, x1_ev[tp])]
            if tp == 0:
                waits.append((eDVE.h, ctxfree))
            else:
                waits.append((eDVE.h, ptcopy_last4))
            return emit_transposes(mksrc4(tp), mkdst4(tp), waits)

        for tb in range(TT):
            @block.sync
            def _(sy, tb=tb):
                if tb == 0:
                    sy.wait_ge(qSA.h, qSA.n)
                    sy.wait_ge(qSB.h, qSB.n)
                if tb >= 2:
                    sy.wait_ge(eDVE.h, x1add_ev[tb - 2])
                xr_ev[tb] = qS_[tb % 2].dma(sy.dma_start(
                    out=xslot(tb), in_=x_in.ap()[tb * 128:(tb + 1) * 128, :]))

            @block.tensor
            def _(t_, tb=tb):
                if tb == 0:
                    t_.wait_ge(*p0free)
                    t_.wait_ge(eDVE.h, ctxfree)
                else:
                    t_.wait_ge(eDVE.h, x1add_ev[tb - 1])
                if HC > 12:
                    hc_order = [12, 13, 14, 15] + list(range(0, 7)) + [8, 9, 10, 11, 7]
                else:
                    hc_order = list(range(HC))
                for oi, hc in enumerate(hc_order):
                    if tb == 0 and oi == 0:
                        w0 = max(woS_ev[h2] for h2 in
                                 list(range(min(7, HC))) + [h for h in (12, 13, 14, 15) if h < HC])
                        t_.wait_ge(qWO.h, w0)
                    if tb == 0 and HC > 12 and hc in (7, 8, 9, 10, 11):
                        t_.wait_ge(qWO.h, woS_ev[hc])
                    for fq in range(FQN):
                        inst = t_.matmul(
                            P0.ap()[:, fq * 512:(fq + 1) * 512],
                            CT.ap()[:, hc * T + tb * 128:hc * T + tb * 128 + 128],
                            woslot(hc)[:, fq * 512:(fq + 1) * 512],
                            start=(oi == 0), stop=(oi == HC - 1))
                op_ev[tb] = ePE.inc(inst)

            @block.vector
            def _(v, tb=tb):
                v.wait_ge(ePE.h, op_ev[tb])
                v.wait_ge(qS_[tb % 2].h, xr_ev[tb])
                if tb >= 1:
                    v.wait_ge(qP1.h, x1st_ev[tb - 1])
                vchain(v, v.tensor_tensor(FSf[:, 0:H], P0.ap()[:, 0:H],
                                          xslot(tb), ALU.add))
                x1add_ev[tb] = eDVE.n

            if tb >= 1:
                x1t_pe_ev[tb - 1], ptcopy_last4 = emit_x1t(tb - 1)

            @block.vector
            def _(v, tb=tb):
                if tb >= 2:
                    v.wait_ge(ePE.h, x1t_pe_ev[tb - 2])
                emit_ln(v, FSf[:, 0:H], xlnslot(tb), scratch=FSf[:, H:2 * H])
                x1_ev[tb] = eDVE.inc(v.engine_nop())

            @block.scalar
            def _(s, tb=tb):
                s.wait_ge(eDVE.h, x1add_ev[tb])
                x1st_ev[tb] = qP1.dma(s.dma_start(
                    out=x1_d.ap()[tb * 128:(tb + 1) * 128, :], in_=FSf[:, 0:H]))

        x1t_pe_ev[TT - 1], ptcopy_last4 = emit_x1t(TT - 1)
        ph4_end_dve = eDVE.n
        ph4_end_pe = ePE.n
        x1st_fence = [(qP1, x1st_ev[TT - 1])]

        IBH = IB // 2
        NTF = T // 512
        assert IB % 2 == 0
        QTH = QT_C // T

        def hslot(j):
            if j < QTH:
                return QT.ap()[:, j * T:(j + 1) * T]
            return CT.ap()[:, (j - QTH) * T:(j - QTH + 1) * T]

        def guw(j, up):
            s_ = ((6 if up else 4) + j) if j < 2 else ((2 if up else 0) + j % 2)
            return WB.ap()[:, s_ * H:(s_ + 1) * H]

        def dwslot(j):
            s_ = (j + 4) % 8
            return WB.ap()[:, s_ * H:(s_ + 1) * H]

        SDSB = WB.ap()[:, 2 * GD * H:2 * GD * H + H]
        yd_sems = [qK_[0][0], qK_[0][1], qK_[1][0], qK_[1][1],
                   qV_[0], qV_[1], qE_[0], qE_[1]]

        @block.gpsimd
        def _(g):
            g.wait_ge(ePE.h, ph4_end_pe)
            qMG.dma(g.dma_start(out=SDSB,
                                in_=bass.AP(down_s, 0, [[0, 128], [1, H]])))
        sdsb_ev = qMG.n

        @block.tensor
        def _(t_):
            t_.wait_ge(eDVE.h, ph4_end_dve)
            for tf in range(NTF):
                for hc in range(HC):
                    inst = t_.matmul(P0.ap()[0:1, tf * 512:tf * 512 + 512], ONEB.ap(),
                                     XT.ap()[:, hc * T + tf * 512:hc * T + tf * 512 + 512],
                                     start=(hc == 0), stop=(hc == HC - 1))
            ePE.inc(inst)
        rsx_pe = ePE.n

        @block.vector
        def _(v):
            v.wait_ge(ePE.h, rsx_pe)
            for (sem_, val_) in x1st_fence:
                v.wait_ge(sem_.h, val_)
            vchain(v, v.tensor_copy(FSf[0:1, 0:T], P0.ap()[0:1, 0:T]))
            vchain(v, v.tensor_copy(FM.ap()[0:1, 0:T], FSf[0:1, 0:T]))
            vchain(v, v.tensor_sub(FM.ap()[0:1, T:2 * T], FSf[0:1, 0:T],
                                   FM.ap()[0:1, 0:T]))
        rsx_dve = eDVE.n

        prev_half_pe = ph4_end_pe
        prev_half_dve = eDVE.n
        prev_half_d6 = 0
        spill_ev = 0
        yadd_ev = {}
        dsems = [qGB, qD0, qD1, qMG]
        for half in range(2):
            draw_ev = {}
            scale_ev = {}
            gul_ev = {}
            gmm_ev = {}
            gu_act = {}
            gu_dve = {}
            gu_ev = {}
            h_ev = {}
            for j in range(IBH):
                ib = half * IBH + j

                @block.sync
                def _(sy, ib=ib, j=j, ph6=prev_half_d6, half=half):
                    if j >= 2:
                        sy.wait_ge(ePE.h, gmm_ev[j - 2])
                    else:
                        sy.wait_ge(qV_[j % 2].h, qV_[j % 2].n)
                        if half == 1:
                            sy.wait_ge(ePE.h, ph6)
                    if half == 0 and j < 8:
                        sy.wait_ge(qPP.h, qpp_tot)
                    elif (half == 0 and j in (8, 9)) or (half == 1 and j < 2):
                        sy.wait_ge(qGA.h, pp_totA)
                        sy.wait_ge(qGB.h, pp_totB)
                    qV_[j % 2].dma(sy.dma_start(
                        out=guw(j, False),
                        in_=gbf_d.ap()[ib * 128:(ib + 1) * 128, :]))
                    gul_ev[j] = qV_[j % 2].dma(sy.dma_start(
                        out=guw(j, True),
                        in_=ubf_d.ap()[ib * 128:(ib + 1) * 128, :]))

                @block.tensor
                def _(t_, ib=ib, j=j, half=half, phd=prev_half_dve):
                    t_.wait_ge(qV_[j % 2].h, gul_ev[j])
                    if j == 0:
                        t_.wait_ge(eDVE.h, rsx_dve)
                    if half == 1 and j == 0:
                        t_.wait_ge(eDVE.h, yadd_ev[(0, -(-IBH // GD) - 1, TT - 2)])
                    if half == 1 and j == 1:
                        t_.wait_ge(eDVE.h, phd)
                    if j >= 2:
                        t_.wait_ge(eACT.h, gu_act[j - 2])
                        t_.wait_ge(eDVE.h, gu_dve[j - 2])
                        t_.wait_ge(ePO.h, gu_ev[j - 2])
                    PS = P0.ap() if j % 2 == 0 else P1.ap()
                    for (base, up_) in ((0, False), (T, True)):
                        w_ap = guw(j, up_)
                        for hc in range(HC):
                            for tf in range(NTF):
                                t_.matmul(
                                    PS[:, base + tf * 512:base + tf * 512 + 512],
                                    w_ap[:, hc * 128:(hc + 1) * 128],
                                    XT.ap()[:, hc * T + tf * 512:hc * T + tf * 512 + 512],
                                    start=(hc == 0), stop=False)
                        for ri in range(2):
                            rrow = FM.ap()[0:1, ri * T:(ri + 1) * T]
                            for tf in range(NTF):
                                inst = t_.matmul(
                                    PS[:, base + tf * 512:base + tf * 512 + 512],
                                    NEGR, rrow[0:1, tf * 512:tf * 512 + 512],
                                    start=False, stop=(ri == 1))
                    gmm_ev[j] = ePE.inc(inst)

                @block.scalar
                def _(s, ib=ib, j=j, half=half, spl=spill_ev):
                    s.wait_ge(ePE.h, gmm_ev[j])
                    if j == 0 and half == 0:
                        for (sem_, val_) in x1st_fence:
                            s.wait_ge(sem_.h, val_)
                    if j == 0 and half == 1 and spl:
                        s.wait_ge(qOUT.h, spl)
                    if j >= 1:
                        s.wait_ge(ePO.h, h_ev[j - 1])
                    PS = P0.ap() if j % 2 == 0 else P1.ap()
                    gu_act[j] = eACT.inc(s.activation(
                        FSf[:, 0:T], PS[:, 0:T], AF.Sigmoid,
                        scale=SGS.ap()[:, ib:ib + 1]))

                @block.vector
                def _(v, ib=ib, j=j, half=half, php=prev_half_pe):
                    if j >= 1:
                        v.wait_ge(ePO.h, gu_ev[j - 1])
                    if j == 0 and half == 1:
                        v.wait_ge(ePE.h, php)
                    v.wait_ge(ePE.h, gmm_ev[j])
                    vchain(v, v.tensor_scalar(FSf[:, 2 * T:3 * T],
                                              (P0.ap() if j % 2 == 0 else P1.ap())[:, T:2 * T],
                                              SUS.ap()[:, ib:ib + 1], None, ALU.mult))
                    v.wait_ge(eACT.h, gu_act[j])
                    gu_dve[j] = eDVE.inc(v.tensor_scalar(
                        FSf[:, T:2 * T],
                        (P0.ap() if j % 2 == 0 else P1.ap())[:, 0:T],
                        SGS.ap()[:, ib:ib + 1], None, ALU.mult))

                @block.gpsimd
                def _(g, j=j, half=half, php=prev_half_pe):
                    g.wait_ge(eDVE.h, gu_dve[j])
                    if j == 0 and half == 1:
                        g.wait_ge(ePE.h, php)
                    gu_ev[j] = ePO.inc(g.tensor_tensor(
                        FSf[:, 3 * T:4 * T], FSf[:, T:2 * T],
                        FSf[:, 2 * T:3 * T], ALU.mult))
                    g.wait_ge(ePO.h, ePO.n)
                    h_ev[j] = ePO.inc(g.tensor_tensor(hslot(j), FSf[:, 3 * T:4 * T],
                                                      FSf[:, 0:T], ALU.mult))
                    if j >= IBH - 4:
                        jd = j - (IBH - 4)
                        sem = dsems[jd % 4]
                        if jd == 0 and half == 0:
                            g.wait_ge(qGA.h, pp_totA)
                            g.wait_ge(qGB.h, pp_totB)
                        draw_ev[jd] = sem.dma(g.dma_start(
                            out=dwslot(jd),
                            in_=dbf_d.ap()[(half * IBH + jd) * 128:
                                           (half * IBH + jd + 1) * 128, :]))
                        g.wait_ge(sem.h, draw_ev[jd])
                        if jd == 0:
                            g.wait_ge(qMG.h, sdsb_ev)
                        scale_ev[jd] = ePO.inc(g.tensor_tensor(
                            dwslot(jd), dwslot(jd), SDSB, ALU.mult))

            @block.tensor
            def _(t_):
                t_.wait_ge(eACT.h, gu_act[IBH - 2])
                t_.wait_ge(eDVE.h, gu_dve[IBH - 2])
                for j in range(IBH):
                    t_.wait_ge(ePO.h, h_ev[j])
                    for tf in range(NTF):
                        inst = t_.matmul(P0.ap()[32:33, tf * 512:tf * 512 + 512],
                                         ONEB.ap(), hslot(j)[:, tf * 512:tf * 512 + 512],
                                         start=(j == 0), stop=(j == IBH - 1),
                                         tile_position=(0, 32))
                ePE.inc(inst)
            rsh_pe = ePE.n

            @block.vector
            def _(v, rsh_pe=rsh_pe):
                v.wait_ge(ePE.h, rsh_pe)
                vchain(v, v.tensor_scalar(FSf[32:33, 0:T], P0.ap()[32:33, 0:T],
                                          -128.0, None, ALU.mult))
                vchain(v, v.tensor_copy(FM.ap()[32:33, 0:T], FSf[32:33, 0:T]))
                vchain(v, v.tensor_sub(FM.ap()[32:33, T:2 * T], FSf[32:33, 0:T],
                                       FM.ap()[32:33, 0:T]))
            rsh_dve = eDVE.n

            yin_ev = {}
            if half == 0:
                @block.sync
                def _(sy, rshd=rsh_dve):
                    sy.wait_ge(eDVE.h, rshd)
                    for tb in range(TT):
                        sem = yd_sems[tb]
                        sy.wait_ge(sem.h, sem.n)
                        yin_ev[tb] = sem.dma(sy.dma_start(
                            out=yacc(tb),
                            in_=x1_d.ap()[tb * 128:(tb + 1) * 128, :]))
            else:
                if TT > 6:
                    @block.sync
                    def _(sy, rshd=rsh_dve, spl=spill_ev):
                        sy.wait_ge(eDVE.h, rshd)
                        sy.wait_ge(qOUT.h, spl)
                        for tb in (6, 7):
                            sem = yd_sems[tb]
                            sy.wait_ge(sem.h, sem.n)
                            yin_ev[tb] = sem.dma(sy.dma_start(
                                out=yacc(tb),
                                in_=y67_d.ap()[(tb - 6) * 128:(tb - 5) * 128, :]))

            groups = [list(range(i, min(i + GD, IBH))) for i in range(0, IBH, GD)]
            ngr = len(groups)
            dmm_ev = {}

            for j in range(4, min(IBH, 8)):
                @block.gpsimd
                def _(g, j=j, half=half):
                    sem = dsems[j % 4]
                    if j == 4:
                        g.wait_ge(ePE.h, gmm_ev[IBH - 1])
                    draw_ev[j] = sem.dma(g.dma_start(
                        out=dwslot(j),
                        in_=dbf_d.ap()[(half * IBH + j) * 128:
                                       (half * IBH + j + 1) * 128, :]))
                    g.wait_ge(sem.h, draw_ev[j])
                    scale_ev[j] = ePO.inc(g.tensor_tensor(
                        dwslot(j), dwslot(j), SDSB, ALU.mult))

            for gi, grp5 in enumerate(groups):
                for j in grp5:
                    if j < 8:
                        continue
                    @block.gpsimd
                    def _(g, j=j, half=half):
                        sem = dsems[j % 4]
                        g.wait_ge(ePE.h, dmm_ev[((j - 8) // GD, TT - 1)])
                        draw_ev[j] = sem.dma(g.dma_start(
                            out=dwslot(j),
                            in_=dbf_d.ap()[(half * IBH + j) * 128:
                                           (half * IBH + j + 1) * 128, :]))
                        g.wait_ge(sem.h, draw_ev[j])
                        scale_ev[j] = ePO.inc(g.tensor_tensor(
                            dwslot(j), dwslot(j), SDSB, ALU.mult))

                for tb in range(TT):
                    @block.tensor
                    def _(t_, gi=gi, grp5=grp5, tb=tb, rshd=rsh_dve, half=half):
                        t_.wait_ge(ePO.h, scale_ev[grp5[-1]])
                        idx = gi * TT + tb
                        ps = P0.ap() if idx % 2 == 0 else P1.ap()
                        if idx < 2:
                            t_.wait_ge(eDVE.h, rshd)
                        if idx >= 2:
                            kg, ktb = divmod(idx - 2, TT)
                            t_.wait_ge(eDVE.h, yadd_ev[(half, kg, ktb)])
                        elif half == 1:
                            t_.wait_ge(eDVE.h, yadd_ev[(0, ngr - 1, TT - 2 + idx)])
                        last = (gi == ngr - 1)
                        ng5 = len(grp5)
                        for i_loc, j in enumerate(grp5):
                            for fq in range(FQN):
                                inst = t_.matmul(
                                    ps[:, fq * 512:(fq + 1) * 512],
                                    hslot(j)[:, tb * 128:(tb + 1) * 128],
                                    dwslot(j)[:, fq * 512:fq * 512 + 512],
                                    start=(i_loc == 0),
                                    stop=(not last and i_loc == ng5 - 1 and fq == FQN - 1))
                        if last:
                            for ri in range(2):
                                rrow = FM.ap()[32:33, ri * T + tb * 128:ri * T + (tb + 1) * 128]
                                for fq in range(FQN):
                                    inst = t_.matmul(
                                        ps[:, fq * 512:(fq + 1) * 512], rrow,
                                        SDSB[32:33, fq * 512:(fq + 1) * 512],
                                        start=False,
                                        stop=(ri == 1 and fq == FQN - 1),
                                        tile_position=(32, 0))
                        dmm_ev[(gi, tb)] = ePE.inc(inst)

                    @block.vector
                    def _(v, gi=gi, tb=tb, half=half):
                        v.wait_ge(ePE.h, dmm_ev[(gi, tb)])
                        if gi == 0 and (half == 0 or (TT > 6 and tb in (6, 7))):
                            v.wait_ge(yd_sems[tb].h, yin_ev[tb])
                        idx = gi * TT + tb
                        ps = P0.ap() if idx % 2 == 0 else P1.ap()
                        yadd_ev[(half, gi, tb)] = eDVE.inc(v.tensor_tensor(
                            yacc(tb), ps[:, 0:H], yacc(tb), ALU.add))

            if half == 0 and TT > 6:
                @block.sync
                def _(sy, ngr=ngr):
                    for tb in (6, 7):
                        sy.wait_ge(eDVE.h, yadd_ev[(0, ngr - 1, tb)])
                    qOUT.dma(sy.dma_start(out=y67_d.ap()[0:128, :], in_=FSf[:, 0:H]))
                    qOUT.dma(sy.dma_start(out=y67_d.ap()[128:256, :], in_=FSf[:, H:2 * H]))
                spill_ev = qOUT.n
            prev_half_pe = ePE.n
            prev_half_dve = eDVE.n
            prev_half_d6 = dmm_ev[(max(ngr - 2, 0), TT - 1)]

        ngr2 = (IBH + GD - 1) // GD

        @block.sync
        def _(sy, ngr2=ngr2):
            for tb in range(TT):
                sy.wait_ge(eDVE.h, yadd_ev[(1, ngr2 - 1, tb)])
                qOUT.dma(sy.dma_start(
                    out=out_d.ap()[tb * 128:(tb + 1) * 128, :], in_=yacc(tb)))

        @block.gpsimd
        def _(g):
            for s_ in [qLD, qSA, qSB, qOUT, qGA, qGB, qMG, qD0, qD1, qP0, qP1, qPP,
                       qK_[0][0], qK_[0][1], qK_[1][0], qK_[1][1],
                       qV_[0], qV_[1], qE_[0], qE_[1], qE_[2], qE_[3]]:
                g.wait_ge(s_.h, s_.n)

    return nc


def _tile4(w):
    n0, n1 = w.shape
    a = w.reshape(n0 // 128, 128, n1 // 128, 128).transpose(0, 3, 2, 1)
    return np.ascontiguousarray(a).reshape(n0 // 128, 128, n1)


def prep_inputs(cfg, inputs):
    S = cfg["S"]; HEADS = cfg["HEADS"]; I = cfg["I"]
    H = 64 * HEADS
    T = S // NCORES
    IB = I // 128
    x = np.ascontiguousarray(np.asarray(inputs["L0_attn_hidden_states"],
                                        dtype=np.float32).reshape(S, H))
    wq = _tile4(np.asarray(inputs["L0_attn_wq"], dtype=np.float32))
    wk = _tile4(np.asarray(inputs["L0_attn_wk"], dtype=np.float32))
    wv = _tile4(np.asarray(inputs["L0_attn_wv"], dtype=np.float32))
    wo2 = np.ascontiguousarray(np.asarray(inputs["L0_attn_wo"], dtype=np.float32).T)
    relp = np.ascontiguousarray(np.asarray(inputs["L0_attn_rel_pos"], dtype=np.float32))
    gt = _tile4(np.asarray(inputs["L0_ffn_gate_w"], dtype=np.int32))
    ut = _tile4(np.asarray(inputs["L0_ffn_up_w"], dtype=np.int32))
    dt_ = np.ascontiguousarray(
        np.asarray(inputs["L0_ffn_down_w"], dtype=np.int32).T).reshape(IB, 128, H)
    gs = np.ascontiguousarray(
        np.asarray(inputs["L0_ffn_gate_scale"], dtype=np.float32).reshape(IB, 128).T)
    us = np.ascontiguousarray(
        np.asarray(inputs["L0_ffn_up_scale"], dtype=np.float32).reshape(IB, 128).T)
    ds = np.ascontiguousarray(
        np.asarray(inputs["L0_ffn_down_scale"], dtype=np.float32).reshape(1, H))
    ident = np.eye(128).astype(ml_dtypes.bfloat16)
    anti = np.ascontiguousarray(np.eye(128)[::-1]).astype(ml_dtypes.bfloat16)

    in_maps = []
    for c in range(NCORES):
        sel = np.zeros((8, 128), np.float32)
        sel[c, :] = 1.0
        in_maps.append({
            "x": np.ascontiguousarray(x[c * T:(c + 1) * T]),
            "wq_t": wq, "wk_t": wk, "wv_t": wv, "wo_t2": wo2,
            "rel_pos": relp,
            "gate_t": gt, "up_t": ut, "down_t": dt_,
            "gate_s": gs, "up_s": us, "down_s": ds,
            "ident": ident, "anti": anti,
            "sel": sel.astype(ml_dtypes.bfloat16),
        })
    return in_maps


def kernel(**inputs):
    cfg = FULL_CFG
    nc = build(cfg)
    in_maps = prep_inputs(cfg, inputs)
    res = run_bass_kernel_spmd(nc, in_maps, core_ids=list(range(NCORES)))
    S = cfg["S"]; H = 64 * cfg["HEADS"]
    out = np.concatenate([r["out"] for r in res.results], axis=0)
    return out.reshape(1, S, H).astype(np.float32)
